# version 04
# baseline (speedup 1.0000x reference)
import contextlib
import numpy as np
import ml_dtypes
import concourse.bass as bass
import concourse.mybir as mybir
from concourse.alu_op_type import AluOpType as OP
from concourse.bass_utils import run_bass_kernel_spmd

F32 = mybir.dt.float32
F16 = mybir.dt.float16
BF16 = mybir.dt.bfloat16
I32 = mybir.dt.int32
AF = mybir.ActivationFunctionType
BF = ml_dtypes.bfloat16

D = 2048
T = 4096
B = 4
NH = 16
HD = 128
NC = 8
TC = 2048
NDT = D // 128
ISC = float(1.0 / np.sqrt(np.float32(HD)))
EXPB = -10.0


def _positions(cls):
    return np.concatenate([np.arange(128 * (2 * a + cls),
                                     128 * (2 * a + cls) + 128)
                           for a in range(16)])


def _rope_tables():
    p = np.arange(HD // 2, dtype=np.float64)
    theta = np.power(10000.0, -(2.0 ** p) / HD)
    pos = np.arange(1, T + 1, dtype=np.float64)[:, None]
    ang = pos * theta
    ang = np.concatenate([ang, ang], axis=1)
    return np.cos(ang).astype(np.float32), np.sin(ang).astype(np.float32)


def _mask_table(cls):
    m = np.zeros((128, 8, 512), np.float32)
    tri = (np.arange(128)[None, :] >= np.arange(128)[:, None])
    for d in range(8):
        for i in range(4):
            rel = 2 * i + cls
            if d < rel:
                m[:, d, i * 128:(i + 1) * 128] = 1.0
            elif d == rel:
                m[:, d, i * 128:(i + 1) * 128] = tri
    return m.reshape(128, 8 * 512).astype(BF)



class Ctx:
    def __init__(self, nc):
        self.nc = nc
        self.es = contextlib.ExitStack()
        self._sems = {}
        self.cnt = {}

    def sem(self, name):
        if name not in self._sems:
            self._sems[name] = self.es.enter_context(self.nc.semaphore(name))
            self.cnt.setdefault(name, 0)
        return self._sems[name]

    def inc(self, instr, name, n):
        instr.then_inc(self.sem(name), n)
        self.cnt[name] = self.cnt.get(name, 0) + n
        return self.cnt[name]

    def wait(self, eng, name, val=None):
        v = self.cnt.get(name, 0) if val is None else val
        if v > 0:
            eng.wait_ge(self.sem(name), v)

    def snap(self, *names):
        return {n: self.cnt.get(n, 0) for n in names}

    def barrier(self, names, engines=None):
        nc = self.nc
        if engines is None:
            engines = (nc.tensor, nc.vector, nc.scalar, nc.gpsimd, nc.sync)
        for e in engines:
            for n in names:
                self.wait(e, n)

    def sb(self, name, shape, dtype):
        return self.es.enter_context(self.nc.sbuf_tensor(name, shape, dtype))

    def ps(self, name, shape=(128, 512), dtype=F32):
        return self.es.enter_context(
            self.nc.psum_tensor(name, list(shape), dtype))



_UNIQ = [0]


def _un(base):
    _UNIQ[0] += 1
    return f"{base}_{_UNIQ[0]}"


def _ln(c, x_sb, xn_sb, psA, psB, eps_ap):
    nc = c.nc
    ones = c.ones
    les = contextlib.ExitStack()
    sq = les.enter_context(nc.sbuf_tensor(_un("ln_sq"), [128, 512 * 2], BF16))
    mean = les.enter_context(nc.sbuf_tensor(_un("ln_mean"), [128, 512], F32))
    msq = les.enter_context(nc.sbuf_tensor(_un("ln_msq"), [128, 512], F32))
    rstd = les.enter_context(nc.sbuf_tensor(_un("ln_rstd"), [128, 512], F32))
    tmp = les.enter_context(nc.sbuf_tensor(_un("ln_tmp"), [128, 512], F32))
    c.ln_es = les
    for tch in range(TC // 512):
        t0 = tch * 512
        for dt in range(NDT):
            xa = x_sb[:, dt * TC + t0: dt * TC + t0 + 512]
            slot = dt % 2
            if dt >= 2:
                c.wait(nc.scalar, "lnmm", c.cnt["lnmm"] - 1)
            c.inc(nc.scalar.activation(sq[:, slot * 512:(slot + 1) * 512], xa,
                                       AF.Square), "lnsq", 1)
            c.wait(nc.tensor, "lnsq")
            c.inc(nc.tensor.matmul(psB[:], ones[:],
                                   sq[:, slot * 512:(slot + 1) * 512],
                                   start=(dt == 0), stop=(dt == NDT - 1)),
                  "lnmm", 1)
            c.inc(nc.tensor.matmul(psA[:], ones[:], xa,
                                   start=(dt == 0), stop=(dt == NDT - 1)),
                  "lnmm", 1)
        c.wait(nc.vector, "lnmm")
        c.inc(nc.vector.tensor_scalar(mean[:], psA[:], 1.0 / D, None, OP.mult),
              "lnv", 1)
        c.inc(nc.vector.tensor_scalar(msq[:], psB[:], 1.0 / D, None, OP.mult),
              "lnv", 1)
        c.inc(nc.vector.tensor_tensor(tmp[:], mean[:], mean[:], OP.mult),
              "lnv", 1)
        c.inc(nc.vector.tensor_tensor(tmp[:], msq[:], tmp[:], OP.subtract),
              "lnv", 1)
        c.wait(nc.scalar, "lnv")
        c.inc(nc.scalar.activation(tmp[:], tmp[:], AF.Ln, bias=eps_ap),
              "lnsc", 1)
        c.inc(nc.scalar.activation(rstd[:], tmp[:], AF.Exp, scale=-0.5),
              "lnsc", 1)
        c.wait(nc.vector, "lnsc")
        for dt in range(NDT):
            xa = x_sb[:, dt * TC + t0: dt * TC + t0 + 512]
            xo = xn_sb[:, dt * TC + t0: dt * TC + t0 + 512]
            c.inc(nc.vector.tensor_tensor(tmp[:], xa, mean[:], OP.subtract),
                  "lnv", 1)
            c.inc(nc.vector.tensor_tensor(xo, tmp[:], rstd[:], OP.mult),
                  "lnv", 1)
        if tch < TC // 512 - 1:
            c.wait(nc.tensor, "lnv")


def _dequant(c, gup_pk, gup_sc, selg, dn_pk, dn_sc, seld,
             gu_scr, gu_src, dn_src, P0):
    nc = c.nc
    des = contextlib.ExitStack()
    pk = des.enter_context(nc.sbuf_tensor("dq_pk", [128, 2816], I32))
    s16 = des.enter_context(nc.sbuf_tensor("dq_s16", [8, 2816], F16))
    s32 = des.enter_context(nc.sbuf_tensor("dq_s32", [8, 2816], F32))
    sel = des.enter_context(nc.sbuf_tensor("dq_sel", [8, 128], F32))
    sbc = des.enter_context(nc.sbuf_tensor("dq_sbc", [128, 2816], F32))
    ni = des.enter_context(nc.sbuf_tensor("dq_ni", [128, 1408], I32))
    mi = des.enter_context(nc.sbuf_tensor("dq_mi", [128, 1408], I32))
    mf = des.enter_context(nc.sbuf_tensor("dq_mf", [128, 1408], F32))
    sf = des.enter_context(nc.sbuf_tensor("dq_sf", [128, 1408], F32))
    r1 = des.enter_context(nc.sbuf_tensor("dq_r1", [128, 1408], F32))
    tA = des.enter_context(nc.sbuf_tensor("dq_tA", [128, 1408], F32))
    w_all = des.enter_context(nc.sbuf_tensor("dq_w", [128, 8 * 2816], BF16))
    c.dq_es = des

    def deq_one(pk_ap, sc_ap, sel_ap, out_ap, P, W):
        nc.sync, nc.vector
        c.inc(nc.sync.dma_start(out=pk[0:P, 0:W], in_=pk_ap), "dq_ld", 16)
        c.inc(nc.sync.dma_start(out=s16[0:8, 0:W], in_=sc_ap), "dq_ld", 16)
        c.inc(nc.sync.dma_start(out=sel[0:8, 0:P], in_=sel_ap), "dq_ld", 16)
        c.wait(nc.vector, "dq_ld")
        c.inc(nc.vector.tensor_copy(s32[0:8, 0:W], s16[0:8, 0:W]), "dq_v", 1)
        c.wait(nc.tensor, "dq_v")
        for off in range(0, W, 512):
            w = min(512, W - off)
            if off > 0:
                c.wait(nc.tensor, "dq_ev")
            c.inc(nc.tensor.matmul(P0[0:P, 0:w], sel[0:8, 0:P],
                                   s32[0:8, off:off + w],
                                   start=True, stop=True), "dq_mm", 1)
            c.wait(nc.scalar, "dq_mm")
            c.inc(nc.scalar.activation(sbc[0:P, off:off + w], P0[0:P, 0:w],
                                       AF.Copy), "dq_ev", 1)
        c.wait(nc.vector, "dq_ev")
        CW = W // 2
        for j in range(8):
            for ck in range(2):
                o0 = ck * CW
                pks = pk[0:P, o0:o0 + CW]
                niS, miS = ni[0:P, 0:CW], mi[0:P, 0:CW]
                mfS, sfS = mf[0:P, 0:CW], sf[0:P, 0:CW]
                r1S, tAS = r1[0:P, 0:CW], tA[0:P, 0:CW]
                nc.vector.tensor_scalar(niS, pks, 4 * j, 15,
                                        OP.logical_shift_right,
                                        OP.bitwise_and)
                nc.vector.tensor_scalar(miS, niS, 7, None, OP.bitwise_and)
                nc.vector.tensor_copy(mfS, miS)
                nc.vector.tensor_scalar(miS, niS, 3, None,
                                        OP.logical_shift_right)
                nc.vector.tensor_copy(sfS, miS)
                nc.vector.tensor_scalar(r1S, mfS, 4.0, None, OP.max)
                nc.vector.tensor_tensor(tAS, mfS, r1S, OP.add)
                nc.vector.tensor_scalar(r1S, mfS, 6.0, None, OP.max)
                nc.vector.tensor_scalar(tAS, tAS, 0.5, -8.0, OP.mult, OP.add)
                nc.vector.tensor_tensor(tAS, tAS, r1S, OP.add)
                nc.vector.tensor_scalar(sfS, sfS, -2.0, 1.0, OP.mult, OP.add)
                nc.vector.tensor_tensor(tAS, tAS, sfS, OP.mult)
                c.inc(nc.vector.tensor_tensor(
                    w_all[0:P, j * W + o0: j * W + o0 + CW],
                    tAS, sbc[0:P, o0:o0 + CW], OP.mult), "dq_v", 1)
        c.wait(nc.sync, "dq_v")
        c.inc(nc.sync.dma_start(out=out_ap, in_=w_all[0:P, 0:8 * W]),
              "dq_st", 16)
        c.wait(nc.sync, "dq_st")
        c.wait(nc.vector, "dq_st")
        c.wait(nc.tensor, "dq_st")

    deq_one(gup_pk[:, :], gup_sc[:, :], selg[:, :], gu_scr[:, :], 128, 2816)
    gu_scr3 = gu_scr.rearrange("(cc ch) m -> ch cc m", ch=4)
    for ch_i in range(4):
        c.inc(nc.sync.dma_start(
            out=gu_src[:, ch_i * 2816:(ch_i + 1) * 2816]
            .rearrange("(cc j) o -> cc j o", j=8),
            in_=gu_scr3[ch_i]), "dq_st", 16)
    deq_one(dn_pk[:, :], dn_sc[:, :], seld[:, :],
            dn_src.rearrange("(cc j) o -> cc (j o)", j=8), 88, 2048)


def _qkv(c, wT, q_dst, k_dst, v_dst, xn_sb, P, rope,
         ropeC=None, ropeS=None):
    nc = c.nc
    b = c.snap("qkv_w", "qkv_mm", "qkv_sc", "qkv_smm", "qkv_rot", "qkv_v",
               "qkv_ev", "qkv_st", "v_w", "v_mm", "v_ev")
    with contextlib.ExitStack() as oes:
        ost = oes.enter_context(nc.sbuf_tensor(_un("ost"), [128, 512 * 3], BF16))
        with contextlib.ExitStack() as qes:
            wst = qes.enter_context(
                nc.sbuf_tensor(_un("wst"), [128, 2 * 16 * 128], BF16))
            if rope:
                ropec = qes.enter_context(
                    nc.sbuf_tensor(_un("ropec"), [128, TC], F32))
                ropes = qes.enter_context(
                    nc.sbuf_tensor(_un("ropes"), [128, TC], F32))
                qsb = qes.enter_context(
                    nc.sbuf_tensor(_un("qsb"), [128, 512 * 2], BF16))
                ra = qes.enter_context(nc.sbuf_tensor(_un("ra"), [128, 512], F32))
                rb = qes.enter_context(nc.sbuf_tensor(_un("rb"), [128, 512], F32))
                c.inc(nc.sync.dma_start(out=ropec[:], in_=ropeC[:, :]),
                      "s_rope", 16)
                c.inc(nc.sync.dma_start(out=ropes[:], in_=ropeS[:, :]),
                      "s_rope", 16)
                c.wait(nc.vector, "s_rope")
            nst = 0
            for DT in range(32):
                slot = DT % 2
                wsl = wst[:, slot * 2048:(slot + 1) * 2048]
                if DT >= 2:
                    c.wait(nc.gpsimd, "qkv_mm", b["qkv_mm"] + (DT - 1) * 64)
                c.inc(nc.gpsimd.dma_start(
                    out=wsl.rearrange("p (kt m) -> p kt m", m=128),
                    in_=wT[:, DT * 128:(DT + 1) * 128]
                    .rearrange("(kt p) m -> p kt m", p=128)), "qkv_w", 16)
                c.wait(nc.tensor, "qkv_w")
                for tch in range(4):
                    pp = P[tch % 2]
                    it = DT * 4 + tch
                    if it >= 2:
                        c.wait(nc.tensor, "qkv_sc", b["qkv_sc"] + it - 1)
                    for kt in range(NDT):
                        c.inc(nc.tensor.matmul(
                            pp[:], wsl[:, kt * 128:(kt + 1) * 128],
                            xn_sb[:, kt * TC + tch * 512:
                                  kt * TC + tch * 512 + 512],
                            start=(kt == 0), stop=(kt == NDT - 1)),
                            "qkv_mm", 1)
                    c.wait(nc.scalar, "qkv_mm")
                    dst = q_dst if DT < 16 else k_dst
                    dd = (DT % 16) * 128
                    if rope:
                        qss = qsb[:, (it % 2) * 512:(it % 2 + 1) * 512]
                        if it >= 2:
                            c.wait(nc.scalar, "qkv_rot", b["qkv_rot"] + it - 1)
                        c.inc(nc.scalar.activation(qss, pp[:], AF.Copy),
                              "qkv_sc", 1)
                        pr = P[4 + it % 2]
                        c.wait(nc.tensor, "qkv_sc")
                        if it >= 2:
                            c.wait(nc.tensor, "qkv_rot", b["qkv_rot"] + it - 1)
                        c.inc(nc.tensor.matmul(pr[:], c.swap_sb[:], qss,
                                               start=True, stop=True),
                              "qkv_smm", 1)
                        osl = ost[:, (nst % 3) * 512:(nst % 3 + 1) * 512]
                        cs = ropec[:, tch * 512:(tch + 1) * 512]
                        sn = ropes[:, tch * 512:(tch + 1) * 512]
                        if nst >= 3:
                            c.wait(nc.vector, "qkv_st",
                                   b["qkv_st"] + (nst - 2) * 16)
                        c.wait(nc.vector, "qkv_sc")
                        c.inc(nc.vector.scalar_tensor_tensor(
                            ra[:], qss, c.bq_sb[:, DT:DT + 1], cs, OP.add,
                            OP.mult), "qkv_v", 1)
                        c.wait(nc.vector, "qkv_smm")
                        c.inc(nc.vector.scalar_tensor_tensor(
                            rb[:], pr[:], c.bqs_sb[:, DT:DT + 1], sn, OP.add,
                            OP.mult), "qkv_rot", 1)
                        c.inc(nc.vector.tensor_tensor(osl[:, :], ra[:], rb[:],
                                                      OP.add), "qkv_ev", 1)
                        c.wait(nc.sync, "qkv_ev")
                    else:
                        osl = ost[:, (nst % 3) * 512:(nst % 3 + 1) * 512]
                        if nst >= 3:
                            c.wait(nc.scalar, "qkv_st",
                                   b["qkv_st"] + (nst - 2) * 16)
                        c.inc(nc.scalar.activation(osl, pp[:], AF.Copy),
                              "qkv_sc", 1)
                        c.wait(nc.sync, "qkv_sc")
                    c.inc(nc.sync.dma_start(
                        out=dst[dd:dd + 128, tch * 512:(tch + 1) * 512],
                        in_=osl[:, :]), "qkv_st", 16)
                    nst += 1
        with contextlib.ExitStack() as ves:
            wvt = ves.enter_context(
                nc.sbuf_tensor(_un("wvt"), [128, 2 * 16 * 512], BF16))
            c.wait(nc.gpsimd, "qkv_mm")
            c.wait(nc.gpsimd, "qkv_ev")
            c.wait(nc.gpsimd, "qkv_sc")
            nvev = 0
            for Nch in range(4):
                slot = Nch % 2
                wv = wvt[:, slot * 16 * 512:(slot + 1) * 16 * 512]
                if Nch >= 2:
                    c.wait(nc.gpsimd, "v_mm", b["v_mm"] + (Nch - 1) * 256)
                c.inc(nc.gpsimd.dma_start(
                    out=wv.rearrange("p (kt m) -> p kt m", m=512),
                    in_=wT[:, 2 * D + Nch * 512: 2 * D + (Nch + 1) * 512]
                    .rearrange("(kt p) m -> p kt m", p=128)), "v_w", 16)
                c.wait(nc.tensor, "v_w")
                for tt in range(16):
                    pp = P[2 + nvev % 2]
                    if nvev >= 2:
                        c.wait(nc.tensor, "v_ev", b["v_ev"] + nvev - 1)
                    for kt in range(NDT):
                        c.inc(nc.tensor.matmul(
                            pp[:],
                            xn_sb[:, kt * TC + tt * 128:
                                  kt * TC + tt * 128 + 128],
                            wv[:, kt * 512:(kt + 1) * 512],
                            start=(kt == 0), stop=(kt == NDT - 1)), "v_mm", 1)
                    c.wait(nc.scalar, "v_mm")
                    vsl = ost[:, (nst % 3) * 512:(nst % 3 + 1) * 512]
                    if nst >= 3:
                        c.wait(nc.scalar, "qkv_st", b["qkv_st"] + (nst - 2) * 16)
                    c.inc(nc.scalar.activation(vsl, pp[:], AF.Copy),
                          "v_ev", 1)
                    c.wait(nc.sync, "v_ev")
                    c.inc(nc.sync.dma_start(
                        out=v_dst[tt * 128:(tt + 1) * 128,
                                  Nch * 512:(Nch + 1) * 512],
                        in_=vsl), "qkv_st", 16)
                    nst += 1
                    nvev += 1
    c.barrier(("qkv_st", "v_ev", "qkv_ev"))


def _gating(c, gate_w, gate_b, gbias, xn_sb, wn_dram, psS, P):
    nc = c.nc
    with contextlib.ExitStack() as ges:
        gw_sb = ges.enter_context(nc.sbuf_tensor("gw_sb", [128, NDT * 7], BF16))
        gl = ges.enter_context(nc.sbuf_tensor("gl", [128, 7], F32))
        gex = ges.enter_context(nc.sbuf_tensor("gex", [128, 7], F32))
        gt1 = ges.enter_context(nc.sbuf_tensor("gt1", [128, 7], F32))
        gt2 = ges.enter_context(nc.sbuf_tensor("gt2", [128, 7], F32))
        gm = ges.enter_context(nc.sbuf_tensor("gm", [128, 1], F32))
        gs = ges.enter_context(nc.sbuf_tensor("gs", [128, 1], F32))
        wn = ges.enter_context(nc.sbuf_tensor("wn", [128, 7], BF16))
        wrs = ges.enter_context(nc.sbuf_tensor("wrs", [1, 7 * 128], BF16))

        c.inc(nc.gpsimd.dma_start(
            out=gw_sb[:].rearrange("p (kt m) -> p kt m", m=7),
            in_=gate_w.rearrange("(kt p) m -> p kt m", p=128)), "g_w", 16)
        c.inc(nc.sync.dma_start(out=c.gb_bc[0:1, :], in_=gate_b[:, :]),
              "g_w", 16)
        c.inc(nc.sync.dma_start(out=c.bb_bc[0:1, :], in_=gbias[:, :]),
              "g_w", 16)
        c.wait(nc.tensor, "g_w")
        c.inc(nc.tensor.matmul(psS[:, 0:7], c.ones32[:], c.gb_bc[0:1, :],
                               start=True, stop=True), "g_mm", 1)
        c.wait(nc.scalar, "g_mm")
        c.inc(nc.scalar.activation(c.gb_bc[:], psS[:, 0:7], AF.Copy),
              "g_ev", 1)
        c.wait(nc.tensor, "g_ev")
        c.inc(nc.tensor.matmul(psS[:, 0:7], c.ones32[:], c.bb_bc[0:1, :],
                               start=True, stop=True), "g_mm", 1)
        c.wait(nc.scalar, "g_mm")
        c.inc(nc.scalar.activation(c.bb_bc[:], psS[:, 0:7], AF.Copy),
              "g_ev", 1)

        c.wait(nc.vector, "g_ev")
        for tt in range(16):
            c.wait(nc.tensor, "g_ev")
            c.wait(nc.tensor, "g_v")
            for kt in range(NDT):
                c.inc(nc.tensor.matmul(
                    psS[:, 0:7],
                    xn_sb[:, kt * TC + tt * 128: kt * TC + tt * 128 + 128],
                    gw_sb[:, kt * 7:(kt + 1) * 7],
                    start=(kt == 0), stop=(kt == NDT - 1)), "g_mm", 1)
            c.wait(nc.vector, "g_mm")

            def vs(instr):
                c.inc(instr, "g_v", 1)
                c.wait(nc.vector, "g_v")
            vs(nc.vector.tensor_tensor(gl[:], psS[:, 0:7], c.gb_bc[:],
                                       OP.add))
            vs(nc.vector.reduce_max(gm[:], gl[:], axis=mybir.AxisListType.X))
            vs(nc.vector.tensor_scalar(gm[:], gm[:], -1.0, None, OP.mult))
            c.wait(nc.scalar, "g_v")
            c.inc(nc.scalar.activation(gex[:], gl[:], AF.Exp,
                                       bias=gm[:, 0:1], accum_out=gs[:]),
                  "g_sc", 1)
            c.wait(nc.vector, "g_sc")
            vs(nc.vector.reciprocal(gs[:], gs[:]))
            vs(nc.vector.tensor_scalar(gex[:], gex[:], gs[:, 0:1], None,
                                       OP.mult))
            vs(nc.vector.tensor_tensor(gl[:], gex[:], c.bb_bc[:], OP.add))
            vs(nc.vector.reduce_max(gm[:], gl[:], axis=mybir.AxisListType.X))
            vs(nc.vector.tensor_scalar(gt1[:], gl[:], gm[:, 0:1], None,
                                       OP.is_equal))
            vs(nc.vector.scalar_tensor_tensor(gt2[:], gt1[:], -1e30,
                                              gl[:], OP.mult, OP.add))
            vs(nc.vector.reduce_max(gm[:], gt2[:], axis=mybir.AxisListType.X))
            vs(nc.vector.tensor_scalar(gt1[:], gt2[:], gm[:, 0:1], None,
                                       OP.is_equal))
            vs(nc.vector.scalar_tensor_tensor(gt2[:], gt1[:], -1e30,
                                              gt2[:], OP.mult, OP.add))
            vs(nc.vector.reduce_max(gm[:], gt2[:], axis=mybir.AxisListType.X))
            vs(nc.vector.tensor_scalar(gt1[:], gl[:], gm[:, 0:1], None,
                                       OP.is_ge))
            vs(nc.vector.tensor_tensor(gt2[:], gex[:], gt1[:], OP.mult))
            vs(nc.vector.reduce_sum(gs[:], gt2[:], axis=mybir.AxisListType.X))
            vs(nc.vector.reciprocal(gs[:], gs[:]))
            vs(nc.vector.tensor_scalar(wn[:], gt2[:], gs[:, 0:1], None,
                                       OP.mult))
            c.wait(nc.tensor, "g_v")
            if tt >= 1:
                c.wait(nc.scalar, "g_trow", tt * 16)
            for e in range(7):
                c.inc(nc.tensor.matmul(
                    psS[0:1, 0:128], wn[:, e:e + 1], c.ident[:],
                    start=True, stop=True), "g_tmm", 1)
                c.wait(nc.scalar, "g_tmm")
                c.inc(nc.scalar.activation(
                    wrs[0:1, e * 128:(e + 1) * 128],
                    psS[0:1, 0:128], AF.Copy), "g_tev", 1)
                c.wait(nc.tensor, "g_tev")
            c.wait(nc.sync, "g_tev")
            c.inc(nc.sync.dma_start(
                out=wn_dram[:, tt * 128:(tt + 1) * 128],
                in_=wrs[0:1, :]), "g_trow", 16)
    c.barrier(("g_v", "g_tev", "g_trow"))


def _moe(c, W1, Wg, W2, Ws1, Wsg, Ws2, wn_dram, xn_sb, x_sb, P,
         experts=(-1, 0, 1, 2, 3, 4, 5, 6)):
    nc = c.nc
    with contextlib.ExitStack() as mes:
        h_sb = mes.enter_context(nc.sbuf_tensor("h_sb", [128, NDT * 1024], BF16))
        w1s = mes.enter_context(nc.sbuf_tensor("w1s", [128, 2 * NDT * 128], BF16))
        wgs = mes.enter_context(nc.sbuf_tensor("wgs", [128, 2 * NDT * 128], BF16))
        w2s = mes.enter_context(nc.sbuf_tensor("w2s", [128, 2 * NDT * 128], BF16))
        sil = mes.enter_context(nc.sbuf_tensor("sil", [128, 1024], F32))
        hgc = mes.enter_context(nc.sbuf_tensor("hgc", [128, 512], F32))
        wbc = mes.enter_context(nc.sbuf_tensor("wbc", [128, TC], F32))
        wrow = mes.enter_context(nc.sbuf_tensor("wrow", [1, TC], BF16))

        n12 = 0
        n2 = 0
        nhe = 0
        nxa = 0
        for ei, e in enumerate(experts):
            if e >= 0:
                c.wait(nc.tensor, "moe_xa", nxa)
                c.wait(nc.sync, "g_trow")
                if e >= 1:
                    c.wait(nc.sync, "wbc_mm", e * 4)
                c.inc(nc.sync.dma_start(out=wrow[:, :],
                                        in_=wn_dram[e:e + 1, :]),
                      "wrow_ld", 16)
                c.wait(nc.tensor, "wrow_ld")
                for tchunk in range(4):
                    c.inc(nc.tensor.matmul(
                        P[4][:], c.ones[0:1, :],
                        wrow[0:1, tchunk * 512:(tchunk + 1) * 512],
                        start=True, stop=True), "wbc_mm", 1)
                    c.wait(nc.scalar, "wbc_mm")
                    c.inc(nc.scalar.activation(
                        wbc[:, tchunk * 512:(tchunk + 1) * 512], P[4][:],
                        AF.Copy), "wbc_ev", 1)
                    c.wait(nc.tensor, "wbc_ev")
            for th in range(2):
                t0 = th * 1024
                omm_prev = c.cnt.get("moe_omm", 0)
                for mg in range(NDT):
                    slot = (n12 // 2) % 2
                    if n12 >= 4:
                        c.wait(nc.gpsimd, "moe_hmm", (n12 // 2 - 1) * 64)
                    src1 = (Ws1 if e < 0 else W1[e])
                    srcg = (Wsg if e < 0 else Wg[e])
                    c.inc(nc.gpsimd.dma_start(
                        out=w1s[:, slot * 2048:(slot + 1) * 2048]
                        .rearrange("p (kt m) -> p kt m", m=128),
                        in_=src1[:, mg * 128:(mg + 1) * 128]
                        .rearrange("(kt p) m -> p kt m", p=128)),
                        "moe_w12", 16)
                    c.inc(nc.gpsimd.dma_start(
                        out=wgs[:, slot * 2048:(slot + 1) * 2048]
                        .rearrange("p (kt m) -> p kt m", m=128),
                        in_=srcg[:, mg * 128:(mg + 1) * 128]
                        .rearrange("(kt p) m -> p kt m", p=128)),
                        "moe_w12", 16)
                    n12 += 2
                    c.wait(nc.tensor, "moe_w12")
                    for tc2 in range(2):
                        tt0 = t0 + tc2 * 512
                        pa, pb = P[tc2 * 2], P[tc2 * 2 + 1]
                        if nhe >= 2:
                            c.wait(nc.tensor, "moe_hev", nhe - 1)
                        for kt in range(NDT):
                            c.inc(nc.tensor.matmul(
                                pa[:],
                                w1s[:, slot * 2048 + kt * 128:
                                    slot * 2048 + (kt + 1) * 128],
                                xn_sb[:, kt * TC + tt0: kt * TC + tt0 + 512],
                                start=(kt == 0), stop=(kt == NDT - 1)),
                                "moe_hmm", 1)
                            c.inc(nc.tensor.matmul(
                                pb[:],
                                wgs[:, slot * 2048 + kt * 128:
                                    slot * 2048 + (kt + 1) * 128],
                                xn_sb[:, kt * TC + tt0: kt * TC + tt0 + 512],
                                start=(kt == 0), stop=(kt == NDT - 1)),
                                "moe_hmm", 1)
                        c.wait(nc.scalar, "moe_hmm")
                        ssl = sil[:, (nhe % 2) * 512:(nhe % 2 + 1) * 512]
                        if nhe >= 2:
                            c.wait(nc.scalar, "moe_v", nhe - 1)
                        c.inc(nc.scalar.activation(ssl, pa[:], AF.Silu),
                              "moe_sc", 1)
                        c.wait(nc.vector, "moe_sc")
                        if nhe == 0 or (mg == 0 and tc2 == 0):
                            c.wait(nc.vector, "moe_omm", omm_prev)
                        hsl = h_sb[:, mg * 1024 + tc2 * 512:
                                   mg * 1024 + (tc2 + 1) * 512]
                        c.inc(nc.vector.tensor_tensor(hgc[:], ssl,
                                                      pb[:], OP.mult),
                              "moe_v", 1)
                        if e >= 0:
                            c.inc(nc.vector.tensor_tensor(
                                hsl, hgc[:], wbc[:, tt0:tt0 + 512],
                                OP.mult), "moe_hev", 1)
                        else:
                            c.inc(nc.vector.tensor_copy(hsl, hgc[:]),
                                  "moe_hev", 1)
                        nhe += 1
                c.wait(nc.tensor, "moe_hev")
                for dt in range(NDT):
                    slot2 = n2 % 2
                    if n2 >= 2:
                        c.wait(nc.gpsimd, "moe_omm", (n2 - 1) * 32)
                    src2 = (Ws2 if e < 0 else W2[e])
                    c.inc(nc.gpsimd.dma_start(
                        out=w2s[:, slot2 * 2048:(slot2 + 1) * 2048]
                        .rearrange("p (kt m) -> p kt m", m=128),
                        in_=src2[:, dt * 128:(dt + 1) * 128]
                        .rearrange("(kt p) m -> p kt m", p=128)),
                        "moe_w2", 16)
                    n2 += 1
                    c.wait(nc.tensor, "moe_w2")
                    for tc2 in range(2):
                        tt0 = t0 + tc2 * 512
                        pc = P[4 + nxa % 2]
                        if nxa >= 2:
                            c.wait(nc.tensor, "moe_xa", nxa - 1)
                        for kt in range(NDT):
                            c.inc(nc.tensor.matmul(
                                pc[:],
                                w2s[:, slot2 * 2048 + kt * 128:
                                    slot2 * 2048 + (kt + 1) * 128],
                                h_sb[:, kt * 1024 + tc2 * 512:
                                     kt * 1024 + (tc2 + 1) * 512],
                                start=(kt == 0), stop=(kt == NDT - 1)),
                                "moe_omm", 1)
                        c.wait(nc.vector, "moe_omm")
                        xsl = x_sb[:, dt * TC + tt0: dt * TC + tt0 + 512]
                        c.inc(nc.vector.tensor_tensor(xsl, xsl, pc[:],
                                                      OP.add), "moe_xa", 1)
                        nxa += 1
    c.barrier(("moe_xa", "moe_omm", "moe_hev"))


def _attn(c, kg, vg, q_dram, o_sb, mask_sb, bv_col,
          psS0, psS1, psO, psSum, kh_sb, vh_sb, q_sb, pt,
          sums_acc, rcp, tls, cc_sems):
    nc = c.nc
    b = c.snap("at_lq", "at_lkv", "at_smm", "at_ex", "at_pt", "at_sum",
               "at_bmm", "at_od")
    for nm, v in cc_sems:
        c.wait(nc.sync, nm, v)
    c.wait(nc.sync, "qkv_st")

    def load_q(h):
        slot = h % 2
        if h >= 2:
            c.wait(nc.sync, "at_od", b["at_od"] + (h - 1) * 4)
        c.inc(nc.sync.dma_start(
            out=q_sb[:, slot * TC:(slot + 1) * TC],
            in_=q_dram[h * 128:(h + 1) * 128, :]), "at_lq", 16)

    def load_kv(h):
        slot = h % 2
        khs = kh_sb[:, slot * 2 * TC:(slot + 1) * 2 * TC]
        vhs = vh_sb[:, slot * 2 * TC:(slot + 1) * 2 * TC]
        if h >= 2:
            c.wait(nc.sync, "at_od", b["at_od"] + (h - 1) * 4)
        c.inc(nc.sync.dma_start(
            out=khs[:, 0:TC], in_=kg[h * 128:(h + 1) * 128, :]), "at_lkv", 16)
        c.inc(nc.sync.dma_start(
            out=khs[:, TC:2 * TC],
            in_=kg[(NH + h) * 128:(NH + h + 1) * 128, :]), "at_lkv", 16)
        c.inc(nc.sync.dma_start(
            out=vhs[:, 0:TC].rearrange("p (s m) -> p s m", m=128),
            in_=vg[0:TC, h * 128:(h + 1) * 128]
            .rearrange("(s p) m -> p s m", p=128)), "at_lkv", 16)
        c.inc(nc.sync.dma_start(
            out=vhs[:, TC:2 * TC].rearrange("p (s m) -> p s m", m=128),
            in_=vg[TC:2 * TC, h * 128:(h + 1) * 128]
            .rearrange("(s p) m -> p s m", p=128)), "at_lkv", 16)

    load_q(0)
    load_kv(0)
    load_q(1)
    load_kv(1)
    nsl = 0
    for h in range(NH):
        if h + 2 < NH:
            load_q(h + 2)
            load_kv(h + 2)
        slot = h % 2
        qh = q_sb[:, slot * TC:(slot + 1) * TC]
        khs = kh_sb[:, slot * 2 * TC:(slot + 1) * 2 * TC]
        vhs = vh_sb[:, slot * 2 * TC:(slot + 1) * 2 * TC]
        c.wait(nc.tensor, "at_lq", b["at_lq"] + (h + 1) * 16)
        c.wait(nc.tensor, "at_lkv", b["at_lkv"] + (h + 1) * 64)
        for jc in range(4):
            S = 8 * jc + 8
            for s in range(S):
                ko = (s % 2) * TC + (s // 2) * 128
                ps = (psS0, psS1)[nsl % 2]
                if nsl >= 2:
                    c.wait(nc.tensor, "at_ex", b["at_ex"] + nsl - 1)
                c.inc(nc.tensor.matmul(
                    ps[:], khs[:, ko:ko + 128],
                    qh[:, jc * 512:(jc + 1) * 512],
                    start=True, stop=True), "at_smm", 1)
                ptile = pt[:, (nsl % 3) * 512:(nsl % 3 + 1) * 512]
                c.wait(nc.scalar, "at_smm")
                if nsl >= 3:
                    c.wait(nc.scalar, "at_pt", b["at_pt"] + nsl - 2)
                c.inc(nc.scalar.activation(
                    ptile, ps[:], AF.Exp, scale=ISC,
                    bias=c.mneg10[:, 0:1]), "at_ex", 1)
                if s >= S - 8:
                    d = s - (S - 8)
                    c.wait(nc.vector, "at_ex")
                    c.inc(nc.vector.tensor_tensor(
                        ptile, ptile, mask_sb[:, d * 512:(d + 1) * 512],
                        OP.mult), "at_msk", 1)
                    c.wait(nc.gpsimd, "at_msk")
                else:
                    c.wait(nc.gpsimd, "at_ex")
                if s == 0:
                    c.wait(nc.gpsimd, "at_bmm")
                    c.inc(nc.gpsimd.tensor_copy(sums_acc[:], ptile),
                          "at_sum", 1)
                else:
                    c.inc(nc.gpsimd.tensor_tensor(
                        sums_acc[:], sums_acc[:], ptile, OP.add), "at_sum", 1)
                c.wait(nc.tensor, "at_sum")
                c.inc(nc.tensor.matmul(
                    psO[:], vhs[:, ko:ko + 128], ptile,
                    start=(s == 0), stop=(s == S - 1)), "at_pt", 1)
                nsl += 1
            c.wait(nc.tensor, "at_ln")
            c.inc(nc.tensor.matmul(psSum[:], c.ones32f[:], sums_acc[:],
                                   start=True, stop=True), "at_bmm", 1)
            c.wait(nc.scalar, "at_bmm")
            c.wait(nc.scalar, "at_od")
            c.inc(nc.scalar.activation(tls[:], psSum[:], AF.Ln), "at_ln", 1)
            c.wait(nc.scalar, "at_ln")
            c.inc(nc.scalar.activation(rcp[:], tls[:], AF.Exp, scale=-1.0),
                  "at_rc", 1)
            c.wait(nc.vector, "at_rc")
            c.wait(nc.vector, "at_pt")
            osl = o_sb[:, h * TC + jc * 512: h * TC + (jc + 1) * 512]
            c.inc(nc.vector.tensor_tensor(tls[:], psO[:], rcp[:], OP.mult),
                  "at_on", 1)
            if bv_col is not None:
                c.inc(nc.vector.tensor_scalar(
                    osl, tls[:], bv_col[:, h:h + 1], None, OP.add),
                    "at_od", 1)
            else:
                c.inc(nc.vector.tensor_copy(osl, tls[:]), "at_od", 1)
            c.wait(nc.tensor, "at_od")
    c.barrier(("at_od", "at_sum", "at_msk"))


def _outproj(c, woT, o_sb, x_sb, P, bo_col):
    nc = c.nc
    b = c.snap("op_w", "op_mm", "op_xa")
    with contextlib.ExitStack() as wes:
        wst = wes.enter_context(nc.sbuf_tensor(_un("wst2"), [128, 2 * 16 * 128],
                                               BF16))
        for DT in range(NDT):
            slot = DT % 2
            wsl = wst[:, slot * 2048:(slot + 1) * 2048]
            if DT >= 2:
                c.wait(nc.gpsimd, "op_mm", b["op_mm"] + (DT - 1) * 64)
            c.inc(nc.gpsimd.dma_start(
                out=wsl.rearrange("p (kt m) -> p kt m", m=128),
                in_=woT[:, DT * 128:(DT + 1) * 128]
                .rearrange("(kt p) m -> p kt m", p=128)), "op_w", 16)
            c.wait(nc.tensor, "op_w")
            for tch in range(4):
                pp = P[4 + tch % 2]
                it = DT * 4 + tch
                if it >= 2:
                    c.wait(nc.tensor, "op_xa", b["op_xa"] + it - 1)
                for kt in range(NDT):
                    c.inc(nc.tensor.matmul(
                        pp[:], wsl[:, kt * 128:(kt + 1) * 128],
                        o_sb[:, kt * TC + tch * 512:
                             kt * TC + tch * 512 + 512],
                        start=(kt == 0), stop=(kt == NDT - 1)), "op_mm", 1)
                c.wait(nc.vector, "op_mm")
                xsl = x_sb[:, DT * TC + tch * 512: DT * TC + tch * 512 + 512]
                if bo_col is not None:
                    c.inc(nc.vector.scalar_tensor_tensor(
                        xsl, pp[:], bo_col[:, DT:DT + 1], xsl, OP.add,
                        OP.add), "op_xa", 1)
                else:
                    c.inc(nc.vector.tensor_tensor(xsl, xsl, pp[:],
                                                  OP.add), "op_xa", 1)
    c.barrier(("op_xa",))


def _l1ffn(c, wguT, wdnT, h_dram, xn_es, xn_sb, x_sb, P, eps6_ap):
    nc = c.nc
    with contextlib.ExitStack() as ges:
        wga = ges.enter_context(nc.sbuf_tensor("wga", [128, 2 * 16 * 128], BF16))
        wgb = ges.enter_context(nc.sbuf_tensor("wgb", [128, 2 * 16 * 128], BF16))
        ssl2 = ges.enter_context(nc.sbuf_tensor("ssl2", [128, 2 * 512], F32))
        hst = ges.enter_context(nc.sbuf_tensor("hst", [128, 3 * 512], BF16))
        c.wait(nc.sync, "cc_gu", 1)
        nh = 0
        for j in range(44):
            slot = j % 2
            wg_ = wga[:, slot * 2048:(slot + 1) * 2048]
            wu_ = wgb[:, slot * 2048:(slot + 1) * 2048]
            if j >= 2:
                c.wait(nc.sync, "gu_mm", (j - 1) * 128)
            c.inc(nc.sync.dma_start(
                out=wg_.rearrange("p (kt m) -> p kt m", m=128),
                in_=wguT[:, j * 128:(j + 1) * 128]
                .rearrange("(kt p) m -> p kt m", p=128)), "gu_w", 16)
            c.inc(nc.sync.dma_start(
                out=wu_.rearrange("p (kt m) -> p kt m", m=128),
                in_=wguT[:, (44 + j) * 128:(45 + j) * 128]
                .rearrange("(kt p) m -> p kt m", p=128)), "gu_w", 16)
            c.wait(nc.tensor, "gu_w")
            for tch in range(4):
                pa, pb = P[(tch % 2) * 2], P[(tch % 2) * 2 + 1]
                if nh >= 2:
                    c.wait(nc.tensor, "gu_hv", nh - 1)
                for kt in range(NDT):
                    c.inc(nc.tensor.matmul(
                        pa[:], wg_[:, kt * 128:(kt + 1) * 128],
                        xn_sb[:, kt * TC + tch * 512:
                              kt * TC + tch * 512 + 512],
                        start=(kt == 0), stop=(kt == NDT - 1)), "gu_mm", 1)
                    c.inc(nc.tensor.matmul(
                        pb[:], wu_[:, kt * 128:(kt + 1) * 128],
                        xn_sb[:, kt * TC + tch * 512:
                              kt * TC + tch * 512 + 512],
                        start=(kt == 0), stop=(kt == NDT - 1)), "gu_mm", 1)
                c.wait(nc.scalar, "gu_mm")
                sslc = ssl2[:, (nh % 2) * 512:(nh % 2 + 1) * 512]
                if nh >= 2:
                    c.wait(nc.scalar, "gu_hv", nh - 1)
                c.inc(nc.scalar.activation(sslc, pa[:], AF.Silu), "gu_sc", 1)
                c.wait(nc.vector, "gu_sc")
                hsl = hst[:, (nh % 3) * 512:(nh % 3 + 1) * 512]
                if nh >= 3:
                    c.wait(nc.vector, "gu_st", (nh - 2) * 16)
                c.inc(nc.vector.tensor_tensor(hsl, sslc, pb[:],
                                              OP.mult), "gu_hv", 1)
                c.wait(nc.sync, "gu_hv")
                c.inc(nc.sync.dma_start(
                    out=h_dram[j * 128:(j + 1) * 128,
                               tch * 512:(tch + 1) * 512],
                    in_=hsl), "gu_st", 16)
                nh += 1
    c.barrier(("gu_st", "gu_hv"))
    xn_es.close()

    with contextlib.ExitStack() as des:
        h_sb = des.enter_context(nc.sbuf_tensor("h44", [128, 44 * 512], BF16))
        wdns = des.enter_context(nc.sbuf_tensor("wdns", [128, 2 * 44 * 128],
                                                BF16))
        y_sb = des.enter_context(nc.sbuf_tensor("y_sb", [128, 16 * 512], BF16))
        sqst = des.enter_context(nc.sbuf_tensor("sqst", [128, 2 * 512], BF16))
        rstd = des.enter_context(nc.sbuf_tensor("rstd2", [128, 512], F32))
        tmp2 = des.enter_context(nc.sbuf_tensor("tmp2", [128, 512], F32))
        c.wait(nc.sync, "cc_dn", 1)
        ndq = 0
        for tau in range(4):
            t0 = tau * 512
            c.wait(nc.scalar, "rm_xa")
            c.wait(nc.sync, "dn_ev", c.cnt.get("dn_ev", 0))
            c.inc(nc.sync.dma_start(
                out=h_sb[:].rearrange("p (kt m) -> p kt m", m=512),
                in_=h_dram[:, t0:t0 + 512]
                .rearrange("(kt p) m -> p kt m", p=128)), "dn_h", 16)
            for dt in range(NDT):
                slot = ndq % 2
                wd_ = wdns[:, slot * 44 * 128:(slot + 1) * 44 * 128]
                if ndq >= 2:
                    c.wait(nc.sync, "dn_mm", (ndq - 1) * 44)
                c.inc(nc.sync.dma_start(
                    out=wd_.rearrange("p (kt m) -> p kt m", m=128),
                    in_=wdnT[:, dt * 128:(dt + 1) * 128]
                    .rearrange("(kt p) m -> p kt m", p=128)), "dn_w", 16)
                c.wait(nc.tensor, "dn_w")
                c.wait(nc.tensor, "dn_h")
                pp = P[4 + ndq % 2]
                if ndq >= 2:
                    c.wait(nc.tensor, "dn_ev", ndq - 1)
                for kt in range(44):
                    c.inc(nc.tensor.matmul(
                        pp[:], wd_[:, kt * 128:(kt + 1) * 128],
                        h_sb[:, kt * 512:(kt + 1) * 512],
                        start=(kt == 0), stop=(kt == 43)), "dn_mm", 1)
                c.wait(nc.scalar, "dn_mm")
                c.inc(nc.scalar.activation(
                    y_sb[:, dt * 512:(dt + 1) * 512], pp[:], AF.Copy),
                    "dn_ev", 1)
                ndq += 1
            for dt in range(NDT):
                sq = sqst[:, (dt % 2) * 512:(dt % 2 + 1) * 512]
                if dt >= 2:
                    c.wait(nc.scalar, "rm_mm", c.cnt["rm_mm"] - 1)
                c.inc(nc.scalar.activation(
                    sq, y_sb[:, dt * 512:(dt + 1) * 512], AF.Square),
                    "rm_sq", 1)
                c.wait(nc.tensor, "rm_sq")
                c.inc(nc.tensor.matmul(P[0][:], c.ones[:], sq,
                                       start=(dt == 0),
                                       stop=(dt == NDT - 1)), "rm_mm", 1)
            c.wait(nc.scalar, "rm_mm")
            c.inc(nc.scalar.activation(tmp2[:], P[0][:], AF.Ln,
                                       scale=1.0 / D, bias=eps6_ap),
                  "rm_ln", 1)
            c.wait(nc.scalar, "rm_ln")
            c.inc(nc.scalar.activation(rstd[:], tmp2[:], AF.Exp,
                                       scale=-0.5), "rm_rs", 1)
            c.wait(nc.vector, "rm_rs")
            for dt in range(NDT):
                xsl = x_sb[:, dt * TC + t0: dt * TC + t0 + 512]
                c.inc(nc.vector.tensor_tensor(
                    tmp2[:], y_sb[:, dt * 512:(dt + 1) * 512], rstd[:],
                    OP.mult), "rm_v", 1)
                c.wait(nc.vector, "rm_v")
                c.inc(nc.vector.tensor_tensor(xsl, xsl, tmp2[:], OP.add),
                      "rm_xa", 1)
            c.wait(nc.tensor, "rm_xa")
    c.barrier(("rm_xa", "dn_ev"))



def build_fused(debug=False):
    nc = bass.Bass("TRN2", target_bir_lowering=False)
    c = Ctx(nc)

    xT = nc.dram_tensor("xT", [D, TC], F32, kind="ExternalInput").ap()
    wqkvT = nc.dram_tensor("wqkvT", [D, 3 * D], F32, kind="ExternalInput").ap()
    bqkv = nc.dram_tensor("bqkv", [128, 48], F32, kind="ExternalInput").ap()
    bqkvS = nc.dram_tensor("bqkvS", [128, 48], F32, kind="ExternalInput").ap()
    ropeC = nc.dram_tensor("ropeC", [128, TC], F32, kind="ExternalInput").ap()
    ropeS = nc.dram_tensor("ropeS", [128, TC], F32, kind="ExternalInput").ap()
    W1 = nc.dram_tensor("W1", [7, D, D], F32, kind="ExternalInput").ap()
    Wg = nc.dram_tensor("Wg", [7, D, D], F32, kind="ExternalInput").ap()
    W2 = nc.dram_tensor("W2", [7, D, D], F32, kind="ExternalInput").ap()
    Ws1 = nc.dram_tensor("Ws1", [D, D], F32, kind="ExternalInput").ap()
    Wsg = nc.dram_tensor("Wsg", [D, D], F32, kind="ExternalInput").ap()
    Ws2 = nc.dram_tensor("Ws2", [D, D], F32, kind="ExternalInput").ap()
    gate_w = nc.dram_tensor("gate_w", [D, 7], F32, kind="ExternalInput").ap()
    gate_b = nc.dram_tensor("gate_b", [1, 7], F32, kind="ExternalInput").ap()
    gbias = nc.dram_tensor("gbias", [1, 7], F32, kind="ExternalInput").ap()
    woT0 = nc.dram_tensor("woT0", [D, D], F32, kind="ExternalInput").ap()
    bo_d = nc.dram_tensor("bo_d", [128, 16], F32, kind="ExternalInput").ap()
    bv_d = nc.dram_tensor("bv_d", [128, 16], F32, kind="ExternalInput").ap()
    wprojT = nc.dram_tensor("wprojT", [D, 3 * D], F32,
                            kind="ExternalInput").ap()
    woT1 = nc.dram_tensor("woT1", [D, D], F32, kind="ExternalInput").ap()
    maskD = nc.dram_tensor("maskD", [128, 8 * 512], BF16,
                           kind="ExternalInput").ap()
    identD = nc.dram_tensor("identD", [128, 128], BF16,
                            kind="ExternalInput").ap()
    swapD = nc.dram_tensor("swapD", [128, 128], BF16,
                           kind="ExternalInput").ap()
    gup_pk = nc.dram_tensor("gup_pk", [128, 2816], I32,
                            kind="ExternalInput").ap()
    gup_sc = nc.dram_tensor("gup_sc", [8, 2816], F16,
                            kind="ExternalInput").ap()
    selg = nc.dram_tensor("selg", [8, 128], F32, kind="ExternalInput").ap()
    dn_pk = nc.dram_tensor("dn_pk", [88, 2048], I32,
                           kind="ExternalInput").ap()
    dn_sc = nc.dram_tensor("dn_sc", [8, 2048], F16, kind="ExternalInput").ap()
    seld = nc.dram_tensor("seld", [8, 88], F32, kind="ExternalInput").ap()

    q_dram = nc.dram_tensor("q_dram", [D, TC], BF16, kind="Internal").ap()
    k_src = [nc.dram_tensor(f"k_src{l}", [D, TC], BF16, kind="Internal").ap()
             for l in range(2)]
    v_src = [nc.dram_tensor(f"v_src{l}", [TC, D], BF16, kind="Internal").ap()
             for l in range(2)]
    kg = [nc.dram_tensor(f"kg{l}", [2 * D, TC], BF16, kind="Internal").ap()
          for l in range(2)]
    vg = [nc.dram_tensor(f"vg{l}", [2 * TC, D], BF16, kind="Internal").ap()
          for l in range(2)]
    wn_dram = nc.dram_tensor("wn_dram", [7, TC], BF16, kind="Internal").ap()
    h_dram = nc.dram_tensor("h_dram", [5632, TC], BF16, kind="Internal").ap()
    gu_scr = nc.dram_tensor("gu_scr", [128, 22528], BF16, kind="Internal").ap()
    gu_src = nc.dram_tensor("gu_src", [256, 11264], BF16, kind="Internal").ap()
    dn_src = nc.dram_tensor("dn_src", [704, 2048], BF16, kind="Internal").ap()
    wguT = nc.dram_tensor("wguT", [2048, 11264], BF16,
                          kind="Internal").ap()
    wdnT = nc.dram_tensor("wdnT", [5632, 2048], BF16,
                          kind="Internal").ap()

    out_xT = nc.dram_tensor("out_xT", [D, TC], F32, kind="ExternalOutput").ap()
    if debug:
        d_w1c = nc.dram_tensor("d_w1c", [D, D], F32, kind="ExternalOutput").ap()
        d_wgc = nc.dram_tensor("d_wgc", [D, D], F32, kind="ExternalOutput").ap()
        d_w2c = nc.dram_tensor("d_w2c", [D, D], F32, kind="ExternalOutput").ap()
        dbg = {nm: nc.dram_tensor(nm, shp, BF16, kind="ExternalOutput").ap()
               for nm, shp in [
                   ("d_xn0", [D, TC]), ("d_q0", [D, TC]), ("d_k0", [D, TC]),
                   ("d_v0", [TC, D]), ("d_xm0", [D, TC]), ("d_o0", [D, TC]),
                   ("d_x1", [D, TC]), ("d_xf1", [D, TC]), ("d_o1", [D, TC]),
                   ("d_wn", [7, TC]),
               ]}

        def dump_sb(nm, sb):
            c.inc(nc.sync.dma_start(
                out=dbg[nm].rearrange("(dt p) t -> p dt t", p=128),
                in_=sb[:].rearrange("p (dt t) -> p dt t", t=TC)), "dbg", 16)
            c.wait(nc.sync, "dbg")

        def dump_dram(nm, src):
            c.inc(nc.sync.dma_start(out=dbg[nm][:, :], in_=src[:, :]),
                  "dbg", 16)
            c.wait(nc.sync, "dbg")

    PAIRS = [[0, 1], [2, 3], [4, 5], [6, 7]]

    with c.es:
        x_sb = c.sb("x_sb", [128, NDT * TC], BF16)
        c.ones = c.sb("ones", [128, 128], BF16)
        c.ident = c.sb("ident", [128, 128], BF16)
        c.swap_sb = c.sb("swap_sb", [128, 128], BF16)
        c.ones32 = c.sb("ones32", [1, 128], F32)
        c.ones32f = c.sb("ones32f", [128, 128], F32)
        c.mneg10 = c.sb("mneg10", [128, 1], F32)
        eps_sb = c.sb("eps_sb", [128, 1], F32)
        eps6_sb = c.sb("eps6_sb", [128, 1], F32)
        c.bq_sb = c.sb("bq_sb", [128, 48], F32)
        c.bqs_sb = c.sb("bqs_sb", [128, 48], F32)
        c.gb_bc = c.sb("gb_bc", [128, 7], F32)
        c.bb_bc = c.sb("bb_bc", [128, 7], F32)
        bo_sb = c.sb("bo_sb", [128, 16], F32)
        bv_sb = c.sb("bv_sb", [128, 16], F32)

        P = [c.ps(f"P{i}") for i in range(6)]
        psS = c.ps("psS", (128, 128))

        blk = c.es.enter_context(nc.Block())

        c.inc(nc.gpsimd.dma_start(
            out=x_sb[:].rearrange("p (dt t) -> p dt t", t=TC),
            in_=xT.rearrange("(dt p) t -> p dt t", p=128)), "s_x", 16)
        c.inc(nc.gpsimd.memset(c.ones[:], 1.0), "s_x", 1)
        c.inc(nc.gpsimd.memset(c.ones32[:], 1.0), "s_x", 1)
        c.inc(nc.gpsimd.memset(c.ones32f[:], 1.0), "s_x", 1)
        c.inc(nc.gpsimd.memset(c.mneg10[:], EXPB), "s_x", 1)
        c.inc(nc.gpsimd.memset(eps_sb[:], 1e-5), "s_x", 1)
        c.inc(nc.gpsimd.memset(eps6_sb[:], 1e-6), "s_x", 1)
        c.inc(nc.sync.dma_start(out=c.ident[:], in_=identD[:, :]), "s_x", 16)
        c.inc(nc.sync.dma_start(out=c.swap_sb[:], in_=swapD[:, :]), "s_x", 16)
        c.inc(nc.sync.dma_start(out=c.bq_sb[:], in_=bqkv[:, :]), "s_x", 16)
        c.inc(nc.sync.dma_start(out=c.bqs_sb[:], in_=bqkvS[:, :]), "s_x", 16)
        c.inc(nc.sync.dma_start(out=bo_sb[:], in_=bo_d[:, :]), "s_x", 16)
        c.inc(nc.sync.dma_start(out=bv_sb[:], in_=bv_d[:, :]), "s_x", 16)

        _dequant(c, gup_pk, gup_sc, selg, dn_pk, dn_sc, seld,
                 gu_scr, gu_src, dn_src, P[0])
        c.dq_es.close()

        xn_es = contextlib.ExitStack()
        xn_sb = xn_es.enter_context(
            nc.sbuf_tensor("xn_sb", [128, NDT * TC], BF16))
        c.barrier(("s_x", "dq_st", "dq_v"))
        _ln(c, x_sb, xn_sb, P[0], P[1], eps_sb[:, 0:1])
        c.ln_es.close()
        c.barrier(("lnv",))
        if debug:
            dump_sb("d_xn0", xn_sb)

        _qkv(c, wqkvT, q_dram, k_src[0], v_src[0], xn_sb, P, rope=True,
             ropeC=ropeC, ropeS=ropeS)
        if debug:
            dump_dram("d_q0", q_dram)
            dump_dram("d_k0", k_src[0])
            dump_dram("d_v0", v_src[0])

        _gating(c, gate_w, gate_b, gbias, xn_sb, wn_dram, psS, P)
        if debug:
            dump_dram("d_wn", wn_dram)
        _moe(c, W1, Wg, W2, Ws1, Wsg, Ws2, wn_dram, xn_sb, x_sb, P)
        xn_es.close()

        c.wait(nc.gpsimd, "qkv_st")
        c.inc(nc.gpsimd.collective_compute(
            "AllGather", mybir.AluOpType.bypass, replica_groups=PAIRS,
            ins=[k_src[0][:, :].opt()], outs=[kg[0][:, :].opt()]),
            "cc_k0", 1)
        c.inc(nc.gpsimd.collective_compute(
            "AllGather", mybir.AluOpType.bypass, replica_groups=PAIRS,
            ins=[v_src[0][:, :].opt()], outs=[vg[0][:, :].opt()]),
            "cc_v0", 1)
        c.wait(nc.gpsimd, "dq_st")
        c.inc(nc.gpsimd.collective_compute(
            "AllGather", mybir.AluOpType.bypass,
            replica_groups=[list(range(NC))],
            ins=[gu_src[:, :].opt()], outs=[wguT[:, :].opt()]), "cc_gu", 1)
        c.inc(nc.gpsimd.collective_compute(
            "AllGather", mybir.AluOpType.bypass,
            replica_groups=[list(range(NC))],
            ins=[dn_src[:, :].opt()], outs=[wdnT[:, :].opt()]), "cc_dn", 1)
        if debug:
            dump_sb("d_xm0", x_sb)

        with contextlib.ExitStack() as aes:
            o_sb = aes.enter_context(
                nc.sbuf_tensor("o_sb", [128, NDT * TC], BF16))
            q_sb = aes.enter_context(nc.sbuf_tensor("q_sb", [128, 2 * TC],
                                                    BF16))
            kh_sb = aes.enter_context(
                nc.sbuf_tensor("kh_sb", [128, 2 * 2 * TC], BF16))
            vh_sb = aes.enter_context(
                nc.sbuf_tensor("vh_sb", [128, 2 * 2 * TC], BF16))
            pt = aes.enter_context(nc.sbuf_tensor("pt", [128, 3 * 512], BF16))
            sums_acc = aes.enter_context(
                nc.sbuf_tensor("sums_acc", [128, 512], F32))
            rcp = aes.enter_context(nc.sbuf_tensor("rcp", [128, 512], F32))
            tls = aes.enter_context(nc.sbuf_tensor("tls", [128, 512], F32))
            mask_sb = aes.enter_context(
                nc.sbuf_tensor("mask_sb", [128, 8 * 512], BF16))
            c.inc(nc.sync.dma_start(out=mask_sb[:], in_=maskD[:, :]),
                  "at_mld", 16)
            c.wait(nc.vector, "at_mld")
            _attn(c, kg[0], vg[0], q_dram, o_sb, mask_sb, bv_sb,
                  P[0], P[1], P[2], P[3], kh_sb, vh_sb, q_sb, pt,
                  sums_acc, rcp, tls, [("cc_k0", 1), ("cc_v0", 1)])
            if debug:
                dump_sb("d_o0", o_sb)
            c.wait(nc.gpsimd, "cc_gu", 1)
            c.wait(nc.gpsimd, "cc_dn", 1)
            _outproj(c, woT0, o_sb, x_sb, P, bo_sb)
        if debug:
            dump_sb("d_x1", x_sb)

        xn_es = contextlib.ExitStack()
        xn_sb = xn_es.enter_context(
            nc.sbuf_tensor("xn_sb1", [128, NDT * TC], BF16))
        _ln(c, x_sb, xn_sb, P[0], P[1], eps_sb[:, 0:1])
        c.ln_es.close()
        c.barrier(("lnv",))

        _qkv(c, wprojT, q_dram, k_src[1], v_src[1], xn_sb, P, rope=False)

        c.wait(nc.gpsimd, "qkv_st")
        c.inc(nc.gpsimd.collective_compute(
            "AllGather", mybir.AluOpType.bypass, replica_groups=PAIRS,
            ins=[k_src[1][:, :].opt()], outs=[kg[1][:, :].opt()]),
            "cc_k1", 1)
        c.inc(nc.gpsimd.collective_compute(
            "AllGather", mybir.AluOpType.bypass, replica_groups=PAIRS,
            ins=[v_src[1][:, :].opt()], outs=[vg[1][:, :].opt()]),
            "cc_v1", 1)

        _l1ffn(c, wguT, wdnT, h_dram, xn_es, xn_sb, x_sb, P,
               eps6_sb[:, 0:1])
        if debug:
            dump_sb("d_xf1", x_sb)

        with contextlib.ExitStack() as aes:
            o_sb = aes.enter_context(
                nc.sbuf_tensor("o_sb1", [128, NDT * TC], BF16))
            q_sb = aes.enter_context(nc.sbuf_tensor("q_sb1", [128, 2 * TC],
                                                    BF16))
            kh_sb = aes.enter_context(
                nc.sbuf_tensor("kh_sb1", [128, 2 * 2 * TC], BF16))
            vh_sb = aes.enter_context(
                nc.sbuf_tensor("vh_sb1", [128, 2 * 2 * TC], BF16))
            pt = aes.enter_context(nc.sbuf_tensor("pt1", [128, 3 * 512], BF16))
            sums_acc = aes.enter_context(
                nc.sbuf_tensor("sums_acc1", [128, 512], F32))
            rcp = aes.enter_context(nc.sbuf_tensor("rcp1", [128, 512], F32))
            tls = aes.enter_context(nc.sbuf_tensor("tls1", [128, 512], F32))
            mask_sb = aes.enter_context(
                nc.sbuf_tensor("mask_sb1", [128, 8 * 512], BF16))
            c.inc(nc.sync.dma_start(out=mask_sb[:], in_=maskD[:, :]),
                  "at_mld", 16)
            c.wait(nc.vector, "at_mld")
            _attn(c, kg[1], vg[1], q_dram, o_sb, mask_sb, None,
                  P[0], P[1], P[2], P[3], kh_sb, vh_sb, q_sb, pt,
                  sums_acc, rcp, tls, [("cc_k1", 1), ("cc_v1", 1)])
            if debug:
                dump_sb("d_o1", o_sb)
            _outproj(c, woT1, o_sb, x_sb, P, None)

        c.wait(nc.gpsimd, "op_xa")
        c.inc(nc.gpsimd.dma_start(
            out=out_xT.rearrange("(dt p) t -> p dt t", p=128),
            in_=x_sb[:].rearrange("p (dt t) -> p dt t", t=TC)), "st_x", 16)
        c.wait(nc.gpsimd, "st_x")
        if debug:
            c.inc(nc.sync.dma_start(out=d_w1c[:, :], in_=W1[3]), "dbg", 16)
            c.inc(nc.sync.dma_start(out=d_wgc[:, :], in_=Wg[3]), "dbg", 16)
            c.inc(nc.sync.dma_start(out=d_w2c[:, :], in_=W2[3]), "dbg", 16)
            c.wait(nc.sync, "dbg")

    return nc



def _prep(inputs):
    cos, sin = _rope_tables()
    sinS = sin.copy()
    sinS[:, :64] *= -1.0
    wqkvT = np.ascontiguousarray(inputs["L0_attn_Wqkv"].T)
    bq = np.ascontiguousarray(inputs["L0_attn_bqkv"].reshape(48, 128).T)
    swp = np.concatenate([np.arange(64, 128), np.arange(64)])
    bqs = np.ascontiguousarray(bq[swp])
    swapM = np.zeros((128, 128), dtype=BF)
    swapM[np.arange(128), swp] = 1.0
    woT0 = np.ascontiguousarray(inputs["L0_attn_Wo"].T)
    bo = np.ascontiguousarray(inputs["L0_attn_bo"].reshape(16, 128).T)
    bv = np.ascontiguousarray(
        inputs["L0_attn_bqkv"][2 * D:].reshape(16, 128).T)
    wprojT = np.ascontiguousarray(inputs["L1_attn_w_project"].T)
    woT1 = np.ascontiguousarray(inputs["L1_attn_w_out"].T)
    masks = [_mask_table(0), _mask_table(1)]

    gupT = np.ascontiguousarray(inputs["L1_ffn_gate_up_packed"].T)
    gus = inputs["L1_ffn_gate_up_scales"]
    dnpT = np.ascontiguousarray(inputs["L1_ffn_down_packed"].T)
    dns = inputs["L1_ffn_down_scales"]
    selg = np.zeros((8, 128), np.float32)
    for p in range(128):
        selg[(p // 64) * 4 + (p % 4), p] = 1.0

    x = inputs["L0_attn_x"]
    maps = []
    for cid in range(NC):
        m, cls = cid // 2, cid % 2
        pos = _positions(cls)
        r = cid
        gpk = np.ascontiguousarray(
            gupT[32 * r:32 * r + 32].reshape(128, 2816))
        gsc = np.ascontiguousarray(gus[2 * r:2 * r + 2].reshape(8, 2816))
        dpk = np.ascontiguousarray(dnpT[88 * r:88 * r + 88])
        g0 = (704 * r) // 128
        dsc = np.zeros((8, 2048), np.float16)
        nsr = min(6, 44 - g0)
        dsc[:nsr] = dns[g0:g0 + nsr]
        sd = np.zeros((8, 88), np.float32)
        for cc in range(88):
            sd[(704 * r + 8 * cc) // 128 - g0, cc] = 1.0
        maps.append({
            "xT": np.ascontiguousarray(x[m][pos].T),
            "wqkvT": wqkvT, "bqkv": bq, "bqkvS": bqs,
            "ropeC": np.ascontiguousarray(cos[pos].T),
            "ropeS": np.ascontiguousarray(sinS[pos].T),
            "W1": inputs["L0_ffn_W1"], "Wg": inputs["L0_ffn_Wg"],
            "W2": inputs["L0_ffn_W2"], "Ws1": inputs["L0_ffn_Ws1"],
            "Wsg": inputs["L0_ffn_Wsg"], "Ws2": inputs["L0_ffn_Ws2"],
            "gate_w": inputs["L0_ffn_gate_w"],
            "gate_b": inputs["L0_ffn_gate_b"].reshape(1, 7),
            "gbias": inputs["L0_ffn_biases"].reshape(1, 7),
            "woT0": woT0, "bo_d": bo, "bv_d": bv,
            "wprojT": wprojT, "woT1": woT1,
            "maskD": masks[cls],
            "identD": np.eye(128, dtype=BF), "swapD": swapM,
            "gup_pk": gpk, "gup_sc": gsc, "selg": selg,
            "dn_pk": dpk, "dn_sc": dsc, "seld": sd,
        })
    return maps


def kernel(**inputs):
    inputs = {k: np.asarray(v) for k, v in inputs.items()}
    nc = build_fused()
    res = run_bass_kernel_spmd(nc, _prep(inputs), core_ids=list(range(NC)))
    out = np.empty((B, T, D), dtype=np.float32)
    for cid in range(NC):
        m, cls = cid // 2, cid % 2
        pos = _positions(cls)
        out[m][pos] = np.asarray(res.results[cid]["out_xT"]).T
    return out


# version 08 (final)
# speedup vs baseline: 1.0360x; 1.0360x over previous
import contextlib
import numpy as np
import ml_dtypes
import concourse.bass as bass
import concourse.mybir as mybir
from concourse.alu_op_type import AluOpType as OP
from concourse.bass_utils import run_bass_kernel_spmd

F32 = mybir.dt.float32
F16 = mybir.dt.float16
BF16 = mybir.dt.bfloat16
I32 = mybir.dt.int32
AF = mybir.ActivationFunctionType
BF = ml_dtypes.bfloat16

D = 2048
T = 4096
B = 4
NH = 16
HD = 128
NC = 8
TC = 2048
NDT = D // 128
ISC = float(1.0 / np.sqrt(np.float32(HD)))
EXPB = -10.0


def _positions(cls):
    return np.concatenate([np.arange(128 * (2 * a + cls),
                                     128 * (2 * a + cls) + 128)
                           for a in range(16)])


def _rope_tables():
    p = np.arange(HD // 2, dtype=np.float64)
    theta = np.power(10000.0, -(2.0 ** p) / HD)
    pos = np.arange(1, T + 1, dtype=np.float64)[:, None]
    ang = pos * theta
    ang = np.concatenate([ang, ang], axis=1)
    return np.cos(ang).astype(np.float32), np.sin(ang).astype(np.float32)


def _mask_table(cls):
    m = np.zeros((128, 8, 512), np.float32)
    tri = (np.arange(128)[None, :] >= np.arange(128)[:, None])
    for d in range(8):
        for i in range(4):
            rel = 2 * i + cls
            if d < rel:
                m[:, d, i * 128:(i + 1) * 128] = 1.0
            elif d == rel:
                m[:, d, i * 128:(i + 1) * 128] = tri
    return m.reshape(128, 8 * 512).astype(BF)



class Ctx:
    def __init__(self, nc):
        self.nc = nc
        self.es = contextlib.ExitStack()
        self._sems = {}
        self.cnt = {}

    def sem(self, name):
        if name not in self._sems:
            self._sems[name] = self.es.enter_context(self.nc.semaphore(name))
            self.cnt.setdefault(name, 0)
        return self._sems[name]

    def inc(self, instr, name, n):
        instr.then_inc(self.sem(name), n)
        self.cnt[name] = self.cnt.get(name, 0) + n
        return self.cnt[name]

    def wait(self, eng, name, val=None):
        v = self.cnt.get(name, 0) if val is None else val
        if v > 0:
            eng.wait_ge(self.sem(name), v)

    def snap(self, *names):
        return {n: self.cnt.get(n, 0) for n in names}

    def barrier(self, names, engines=None):
        nc = self.nc
        if engines is None:
            engines = (nc.tensor, nc.vector, nc.scalar, nc.gpsimd, nc.sync)
        for e in engines:
            for n in names:
                self.wait(e, n)

    def sb(self, name, shape, dtype):
        return self.es.enter_context(self.nc.sbuf_tensor(name, shape, dtype))

    def ps(self, name, shape=(128, 512), dtype=F32):
        return self.es.enter_context(
            self.nc.psum_tensor(name, list(shape), dtype))



_UNIQ = [0]


def _un(base):
    _UNIQ[0] += 1
    return f"{base}_{_UNIQ[0]}"


def _ln(c, x_sb, xn_sb, psA, psB, eps_ap):
    nc = c.nc
    ones = c.ones
    les = contextlib.ExitStack()
    sq = les.enter_context(nc.sbuf_tensor(_un("ln_sq"), [128, 512 * 2], BF16))
    mean = les.enter_context(nc.sbuf_tensor(_un("ln_mean"), [128, 512], F32))
    msq = les.enter_context(nc.sbuf_tensor(_un("ln_msq"), [128, 512], F32))
    rstd = les.enter_context(nc.sbuf_tensor(_un("ln_rstd"), [128, 512], F32))
    tmp = les.enter_context(nc.sbuf_tensor(_un("ln_tmp"), [128, 512], F32))
    c.ln_es = les
    for tch in range(TC // 512):
        t0 = tch * 512
        for dt in range(NDT):
            xa = x_sb[:, dt * TC + t0: dt * TC + t0 + 512]
            slot = dt % 2
            if dt >= 2:
                c.wait(nc.scalar, "lnmm", c.cnt["lnmm"] - 1)
            c.inc(nc.scalar.activation(sq[:, slot * 512:(slot + 1) * 512], xa,
                                       AF.Square), "lnsq", 1)
            c.wait(nc.tensor, "lnsq")
            c.inc(nc.tensor.matmul(psB[:], ones[:],
                                   sq[:, slot * 512:(slot + 1) * 512],
                                   start=(dt == 0), stop=(dt == NDT - 1)),
                  "lnmm", 1)
            c.inc(nc.tensor.matmul(psA[:], ones[:], xa,
                                   start=(dt == 0), stop=(dt == NDT - 1)),
                  "lnmm", 1)
        c.wait(nc.vector, "lnmm")
        c.inc(nc.vector.tensor_scalar(mean[:], psA[:], 1.0 / D, None, OP.mult),
              "lnv", 1)
        c.inc(nc.vector.tensor_scalar(msq[:], psB[:], 1.0 / D, None, OP.mult),
              "lnv", 1)
        c.inc(nc.vector.tensor_tensor(tmp[:], mean[:], mean[:], OP.mult),
              "lnv", 1)
        c.inc(nc.vector.tensor_tensor(tmp[:], msq[:], tmp[:], OP.subtract),
              "lnv", 1)
        c.wait(nc.scalar, "lnv")
        c.inc(nc.scalar.activation(tmp[:], tmp[:], AF.Ln, bias=eps_ap),
              "lnsc", 1)
        c.inc(nc.scalar.activation(rstd[:], tmp[:], AF.Exp, scale=-0.5),
              "lnsc", 1)
        c.wait(nc.vector, "lnsc")
        for dt in range(NDT):
            xa = x_sb[:, dt * TC + t0: dt * TC + t0 + 512]
            xo = xn_sb[:, dt * TC + t0: dt * TC + t0 + 512]
            c.inc(nc.vector.tensor_tensor(tmp[:], xa, mean[:], OP.subtract),
                  "lnv", 1)
            c.inc(nc.vector.tensor_tensor(xo, tmp[:], rstd[:], OP.mult),
                  "lnv", 1)
        if tch < TC // 512 - 1:
            c.wait(nc.tensor, "lnv")


def _dequant(c, gup_pk, gup_sc, selg, dn_pk, dn_sc, seld,
             gu_scr, gu_src, dn_src, P0):
    nc = c.nc
    des = contextlib.ExitStack()
    pk = des.enter_context(nc.sbuf_tensor("dq_pk", [128, 2816], I32))
    s16 = des.enter_context(nc.sbuf_tensor("dq_s16", [8, 2816], F16))
    s32 = des.enter_context(nc.sbuf_tensor("dq_s32", [8, 2816], F32))
    sel = des.enter_context(nc.sbuf_tensor("dq_sel", [8, 128], F32))
    sbc = des.enter_context(nc.sbuf_tensor("dq_sbc", [128, 2816], F32))
    ni = des.enter_context(nc.sbuf_tensor("dq_ni", [128, 1408], I32))
    mi = des.enter_context(nc.sbuf_tensor("dq_mi", [128, 1408], I32))
    mf = des.enter_context(nc.sbuf_tensor("dq_mf", [128, 1408], F32))
    sf = des.enter_context(nc.sbuf_tensor("dq_sf", [128, 1408], F32))
    r1 = des.enter_context(nc.sbuf_tensor("dq_r1", [128, 1408], F32))
    tA = des.enter_context(nc.sbuf_tensor("dq_tA", [128, 1408], F32))
    w_all = des.enter_context(nc.sbuf_tensor("dq_w", [128, 8 * 2816], BF16))
    c.dq_es = des

    def deq_one(pk_ap, sc_ap, sel_ap, out_ap, P, W):
        nc.sync, nc.vector
        c.inc(nc.sync.dma_start(out=pk[0:P, 0:W], in_=pk_ap), "dq_ld", 16)
        c.inc(nc.sync.dma_start(out=s16[0:8, 0:W], in_=sc_ap), "dq_ld", 16)
        c.inc(nc.sync.dma_start(out=sel[0:8, 0:P], in_=sel_ap), "dq_ld", 16)
        c.wait(nc.vector, "dq_ld")
        c.inc(nc.vector.tensor_copy(s32[0:8, 0:W], s16[0:8, 0:W]), "dq_v", 1)
        c.wait(nc.tensor, "dq_v")
        for off in range(0, W, 512):
            w = min(512, W - off)
            if off > 0:
                c.wait(nc.tensor, "dq_ev")
            c.inc(nc.tensor.matmul(P0[0:P, 0:w], sel[0:8, 0:P],
                                   s32[0:8, off:off + w],
                                   start=True, stop=True), "dq_mm", 1)
            c.wait(nc.scalar, "dq_mm")
            c.inc(nc.scalar.activation(sbc[0:P, off:off + w], P0[0:P, 0:w],
                                       AF.Copy), "dq_ev", 1)
        c.wait(nc.vector, "dq_ev")
        CW = W // 2
        for j in range(8):
            for ck in range(2):
                o0 = ck * CW
                pks = pk[0:P, o0:o0 + CW]
                niS, miS = ni[0:P, 0:CW], mi[0:P, 0:CW]
                mfS, sfS = mf[0:P, 0:CW], sf[0:P, 0:CW]
                r1S, tAS = r1[0:P, 0:CW], tA[0:P, 0:CW]
                nc.vector.tensor_scalar(niS, pks, 4 * j, 15,
                                        OP.logical_shift_right,
                                        OP.bitwise_and)
                nc.vector.tensor_scalar(miS, niS, 7, None, OP.bitwise_and)
                nc.vector.tensor_copy(mfS, miS)
                nc.vector.tensor_scalar(miS, niS, 3, None,
                                        OP.logical_shift_right)
                nc.vector.tensor_copy(sfS, miS)
                nc.vector.tensor_scalar(r1S, mfS, 4.0, None, OP.max)
                nc.vector.tensor_tensor(tAS, mfS, r1S, OP.add)
                nc.vector.tensor_scalar(r1S, mfS, 6.0, None, OP.max)
                nc.vector.tensor_scalar(tAS, tAS, 0.5, -8.0, OP.mult, OP.add)
                nc.vector.tensor_tensor(tAS, tAS, r1S, OP.add)
                nc.vector.tensor_scalar(sfS, sfS, -2.0, 1.0, OP.mult, OP.add)
                nc.vector.tensor_tensor(tAS, tAS, sfS, OP.mult)
                c.inc(nc.vector.tensor_tensor(
                    w_all[0:P, j * W + o0: j * W + o0 + CW],
                    tAS, sbc[0:P, o0:o0 + CW], OP.mult), "dq_v", 1)
        c.wait(nc.sync, "dq_v")
        c.inc(nc.sync.dma_start(out=out_ap, in_=w_all[0:P, 0:8 * W]),
              "dq_st", 16)
        c.wait(nc.sync, "dq_st")
        c.wait(nc.vector, "dq_st")
        c.wait(nc.tensor, "dq_st")

    deq_one(gup_pk[:, :], gup_sc[:, :], selg[:, :], gu_scr[:, :], 128, 2816)
    gu_scr3 = gu_scr.rearrange("(cc ch) m -> ch cc m", ch=4)
    for ch_i in range(4):
        c.inc(nc.sync.dma_start(
            out=gu_src[:, ch_i * 2816:(ch_i + 1) * 2816]
            .rearrange("(cc j) o -> cc j o", j=8),
            in_=gu_scr3[ch_i]), "dq_st", 16)
    deq_one(dn_pk[:, :], dn_sc[:, :], seld[:, :],
            dn_src.rearrange("(cc j) o -> cc (j o)", j=8), 88, 2048)


def _qkv(c, wT, q_dst, k_dst, v_dst, xn_sb, P, rope,
         ropeC=None, ropeS=None):
    nc = c.nc
    b = c.snap("qkv_w", "qkv_mm", "qkv_sc", "qkv_smm", "qkv_rot", "qkv_v",
               "qkv_ev", "qkv_st", "v_w", "v_mm", "v_ev")
    with contextlib.ExitStack() as oes:
        ost = oes.enter_context(nc.sbuf_tensor(_un("ost"), [128, 512 * 3], BF16))
        with contextlib.ExitStack() as qes:
            wst = qes.enter_context(
                nc.sbuf_tensor(_un("wst"), [128, 2 * 16 * 128], BF16))
            if rope:
                ropec = qes.enter_context(
                    nc.sbuf_tensor(_un("ropec"), [128, TC], F32))
                ropes = qes.enter_context(
                    nc.sbuf_tensor(_un("ropes"), [128, TC], F32))
                qsb = qes.enter_context(
                    nc.sbuf_tensor(_un("qsb"), [128, 512 * 2], BF16))
                ra = qes.enter_context(nc.sbuf_tensor(_un("ra"), [128, 512], F32))
                rb = qes.enter_context(nc.sbuf_tensor(_un("rb"), [128, 512], F32))
                c.inc(nc.sync.dma_start(out=ropec[:], in_=ropeC[:, :]),
                      "s_rope", 16)
                c.inc(nc.sync.dma_start(out=ropes[:], in_=ropeS[:, :]),
                      "s_rope", 16)
                c.wait(nc.vector, "s_rope")
            nst = 0
            for DT in range(32):
                slot = DT % 2
                wsl = wst[:, slot * 2048:(slot + 1) * 2048]
                if DT >= 2:
                    c.wait(nc.gpsimd, "qkv_mm", b["qkv_mm"] + (DT - 1) * 64)
                c.inc(nc.gpsimd.dma_start(
                    out=wsl.rearrange("p (kt m) -> p kt m", m=128),
                    in_=wT[:, DT * 128:(DT + 1) * 128]
                    .rearrange("(kt p) m -> p kt m", p=128)), "qkv_w", 16)
                c.wait(nc.tensor, "qkv_w")
                for tch in range(4):
                    pp = P[tch % 2]
                    it = DT * 4 + tch
                    if it >= 2:
                        c.wait(nc.tensor, "qkv_sc", b["qkv_sc"] + it - 1)
                    for kt in range(NDT):
                        c.inc(nc.tensor.matmul(
                            pp[:], wsl[:, kt * 128:(kt + 1) * 128],
                            xn_sb[:, kt * TC + tch * 512:
                                  kt * TC + tch * 512 + 512],
                            start=(kt == 0), stop=(kt == NDT - 1)),
                            "qkv_mm", 1)
                    c.wait(nc.scalar, "qkv_mm")
                    dst = q_dst if DT < 16 else k_dst
                    dd = (DT % 16) * 128
                    if rope:
                        qss = qsb[:, (it % 2) * 512:(it % 2 + 1) * 512]
                        if it >= 2:
                            c.wait(nc.scalar, "qkv_rot", b["qkv_rot"] + it - 1)
                        c.inc(nc.scalar.activation(qss, pp[:], AF.Copy),
                              "qkv_sc", 1)
                        pr = P[4 + it % 2]
                        c.wait(nc.tensor, "qkv_sc")
                        if it >= 2:
                            c.wait(nc.tensor, "qkv_rot", b["qkv_rot"] + it - 1)
                        c.inc(nc.tensor.matmul(pr[:], c.swap_sb[:], qss,
                                               start=True, stop=True),
                              "qkv_smm", 1)
                        osl = ost[:, (nst % 3) * 512:(nst % 3 + 1) * 512]
                        cs = ropec[:, tch * 512:(tch + 1) * 512]
                        sn = ropes[:, tch * 512:(tch + 1) * 512]
                        if nst >= 3:
                            c.wait(nc.vector, "qkv_st",
                                   b["qkv_st"] + (nst - 2) * 16)
                        c.wait(nc.vector, "qkv_sc")
                        c.inc(nc.vector.scalar_tensor_tensor(
                            ra[:], qss, c.bq_sb[:, DT:DT + 1], cs, OP.add,
                            OP.mult), "qkv_v", 1)
                        c.wait(nc.vector, "qkv_smm")
                        c.inc(nc.vector.scalar_tensor_tensor(
                            rb[:], pr[:], c.bqs_sb[:, DT:DT + 1], sn, OP.add,
                            OP.mult), "qkv_rot", 1)
                        c.inc(nc.vector.tensor_tensor(osl[:, :], ra[:], rb[:],
                                                      OP.add), "qkv_ev", 1)
                        c.wait(nc.sync, "qkv_ev")
                    else:
                        osl = ost[:, (nst % 3) * 512:(nst % 3 + 1) * 512]
                        if nst >= 3:
                            c.wait(nc.scalar, "qkv_st",
                                   b["qkv_st"] + (nst - 2) * 16)
                        c.inc(nc.scalar.activation(osl, pp[:], AF.Copy),
                              "qkv_sc", 1)
                        c.wait(nc.sync, "qkv_sc")
                    c.inc(nc.sync.dma_start(
                        out=dst[dd:dd + 128, tch * 512:(tch + 1) * 512],
                        in_=osl[:, :]), "qkv_st", 16)
                    nst += 1
        with contextlib.ExitStack() as ves:
            wvt = ves.enter_context(
                nc.sbuf_tensor(_un("wvt"), [128, 2 * 16 * 512], BF16))
            c.wait(nc.gpsimd, "qkv_mm")
            c.wait(nc.gpsimd, "qkv_ev")
            c.wait(nc.gpsimd, "qkv_sc")
            nvev = 0
            for Nch in range(4):
                slot = Nch % 2
                wv = wvt[:, slot * 16 * 512:(slot + 1) * 16 * 512]
                if Nch >= 2:
                    c.wait(nc.gpsimd, "v_mm", b["v_mm"] + (Nch - 1) * 256)
                c.inc(nc.gpsimd.dma_start(
                    out=wv.rearrange("p (kt m) -> p kt m", m=512),
                    in_=wT[:, 2 * D + Nch * 512: 2 * D + (Nch + 1) * 512]
                    .rearrange("(kt p) m -> p kt m", p=128)), "v_w", 16)
                c.wait(nc.tensor, "v_w")
                for tt in range(16):
                    pp = P[2 + nvev % 2]
                    if nvev >= 2:
                        c.wait(nc.tensor, "v_ev", b["v_ev"] + nvev - 1)
                    for kt in range(NDT):
                        c.inc(nc.tensor.matmul(
                            pp[:],
                            xn_sb[:, kt * TC + tt * 128:
                                  kt * TC + tt * 128 + 128],
                            wv[:, kt * 512:(kt + 1) * 512],
                            start=(kt == 0), stop=(kt == NDT - 1)), "v_mm", 1)
                    c.wait(nc.scalar, "v_mm")
                    vsl = ost[:, (nst % 3) * 512:(nst % 3 + 1) * 512]
                    if nst >= 3:
                        c.wait(nc.scalar, "qkv_st", b["qkv_st"] + (nst - 2) * 16)
                    c.inc(nc.scalar.activation(vsl, pp[:], AF.Copy),
                          "v_ev", 1)
                    c.wait(nc.sync, "v_ev")
                    c.inc(nc.sync.dma_start(
                        out=v_dst[tt * 128:(tt + 1) * 128,
                                  Nch * 512:(Nch + 1) * 512],
                        in_=vsl), "qkv_st", 16)
                    nst += 1
                    nvev += 1
    c.barrier(("qkv_st", "v_ev", "qkv_ev"))


def _gating(c, gate_w, gate_b, gbias, xn_sb, wn_dram, psS, P):
    nc = c.nc
    with contextlib.ExitStack() as ges:
        gw_sb = ges.enter_context(nc.sbuf_tensor("gw_sb", [128, NDT * 7], BF16))
        gl = ges.enter_context(nc.sbuf_tensor("gl", [128, 7], F32))
        gex = ges.enter_context(nc.sbuf_tensor("gex", [128, 7], F32))
        gt1 = ges.enter_context(nc.sbuf_tensor("gt1", [128, 7], F32))
        gt2 = ges.enter_context(nc.sbuf_tensor("gt2", [128, 7], F32))
        gm = ges.enter_context(nc.sbuf_tensor("gm", [128, 1], F32))
        gs = ges.enter_context(nc.sbuf_tensor("gs", [128, 1], F32))
        wn = ges.enter_context(nc.sbuf_tensor("wn", [128, 7], BF16))
        wrs = ges.enter_context(nc.sbuf_tensor("wrs", [1, 7 * 128], BF16))

        c.inc(nc.gpsimd.dma_start(
            out=gw_sb[:].rearrange("p (kt m) -> p kt m", m=7),
            in_=gate_w.rearrange("(kt p) m -> p kt m", p=128)), "g_w", 16)
        c.inc(nc.sync.dma_start(out=c.gb_bc[0:1, :], in_=gate_b[:, :]),
              "g_w", 16)
        c.inc(nc.sync.dma_start(out=c.bb_bc[0:1, :], in_=gbias[:, :]),
              "g_w", 16)
        c.wait(nc.tensor, "g_w")
        c.inc(nc.tensor.matmul(psS[:, 0:7], c.ones32[:], c.gb_bc[0:1, :],
                               start=True, stop=True), "g_mm", 1)
        c.wait(nc.scalar, "g_mm")
        c.inc(nc.scalar.activation(c.gb_bc[:], psS[:, 0:7], AF.Copy),
              "g_ev", 1)
        c.wait(nc.tensor, "g_ev")
        c.inc(nc.tensor.matmul(psS[:, 0:7], c.ones32[:], c.bb_bc[0:1, :],
                               start=True, stop=True), "g_mm", 1)
        c.wait(nc.scalar, "g_mm")
        c.inc(nc.scalar.activation(c.bb_bc[:], psS[:, 0:7], AF.Copy),
              "g_ev", 1)

        c.wait(nc.vector, "g_ev")
        for tt in range(16):
            c.wait(nc.tensor, "g_ev")
            c.wait(nc.tensor, "g_v")
            for kt in range(NDT):
                c.inc(nc.tensor.matmul(
                    psS[:, 0:7],
                    xn_sb[:, kt * TC + tt * 128: kt * TC + tt * 128 + 128],
                    gw_sb[:, kt * 7:(kt + 1) * 7],
                    start=(kt == 0), stop=(kt == NDT - 1)), "g_mm", 1)
            c.wait(nc.vector, "g_mm")

            def vs(instr):
                c.inc(instr, "g_v", 1)
                c.wait(nc.vector, "g_v")
            vs(nc.vector.tensor_tensor(gl[:], psS[:, 0:7], c.gb_bc[:],
                                       OP.add))
            vs(nc.vector.reduce_max(gm[:], gl[:], axis=mybir.AxisListType.X))
            vs(nc.vector.tensor_scalar(gm[:], gm[:], -1.0, None, OP.mult))
            c.wait(nc.scalar, "g_v")
            c.inc(nc.scalar.activation(gex[:], gl[:], AF.Exp,
                                       bias=gm[:, 0:1], accum_out=gs[:]),
                  "g_sc", 1)
            c.wait(nc.vector, "g_sc")
            vs(nc.vector.reciprocal(gs[:], gs[:]))
            vs(nc.vector.tensor_scalar(gex[:], gex[:], gs[:, 0:1], None,
                                       OP.mult))
            vs(nc.vector.tensor_tensor(gl[:], gex[:], c.bb_bc[:], OP.add))
            vs(nc.vector.reduce_max(gm[:], gl[:], axis=mybir.AxisListType.X))
            vs(nc.vector.tensor_scalar(gt1[:], gl[:], gm[:, 0:1], None,
                                       OP.is_equal))
            vs(nc.vector.scalar_tensor_tensor(gt2[:], gt1[:], -1e30,
                                              gl[:], OP.mult, OP.add))
            vs(nc.vector.reduce_max(gm[:], gt2[:], axis=mybir.AxisListType.X))
            vs(nc.vector.tensor_scalar(gt1[:], gt2[:], gm[:, 0:1], None,
                                       OP.is_equal))
            vs(nc.vector.scalar_tensor_tensor(gt2[:], gt1[:], -1e30,
                                              gt2[:], OP.mult, OP.add))
            vs(nc.vector.reduce_max(gm[:], gt2[:], axis=mybir.AxisListType.X))
            vs(nc.vector.tensor_scalar(gt1[:], gl[:], gm[:, 0:1], None,
                                       OP.is_ge))
            vs(nc.vector.tensor_tensor(gt2[:], gex[:], gt1[:], OP.mult))
            vs(nc.vector.reduce_sum(gs[:], gt2[:], axis=mybir.AxisListType.X))
            vs(nc.vector.reciprocal(gs[:], gs[:]))
            vs(nc.vector.tensor_scalar(wn[:], gt2[:], gs[:, 0:1], None,
                                       OP.mult))
            c.wait(nc.tensor, "g_v")
            if tt >= 1:
                c.wait(nc.scalar, "g_trow", tt * 16)
            for e in range(7):
                c.inc(nc.tensor.matmul(
                    psS[0:1, 0:128], wn[:, e:e + 1], c.ident[:],
                    start=True, stop=True), "g_tmm", 1)
                c.wait(nc.scalar, "g_tmm")
                c.inc(nc.scalar.activation(
                    wrs[0:1, e * 128:(e + 1) * 128],
                    psS[0:1, 0:128], AF.Copy), "g_tev", 1)
                c.wait(nc.tensor, "g_tev")
            c.wait(nc.sync, "g_tev")
            c.inc(nc.sync.dma_start(
                out=wn_dram[:, tt * 128:(tt + 1) * 128],
                in_=wrs[0:1, :]), "g_trow", 16)
    c.barrier(("g_v", "g_tev", "g_trow"))


def _moe(c, W1, Wg, W2, Ws1, Wsg, Ws2, wn_dram, xn_sb, x_sb, P,
         experts=(-1, 0, 1, 2, 3, 4, 5, 6), tick=None):
    nc = c.nc
    with contextlib.ExitStack() as mes:
        h_sb = mes.enter_context(nc.sbuf_tensor("h_sb", [128, NDT * 1024], BF16))
        w1s = mes.enter_context(nc.sbuf_tensor("w1s", [128, 2 * NDT * 128], BF16))
        wgs = mes.enter_context(nc.sbuf_tensor("wgs", [128, 2 * NDT * 128], BF16))
        w2s = mes.enter_context(nc.sbuf_tensor("w2s", [128, 2 * NDT * 128], BF16))
        sil = mes.enter_context(nc.sbuf_tensor("sil", [128, 1024], F32))
        hgc = mes.enter_context(nc.sbuf_tensor("hgc", [128, 512], F32))
        wbc = mes.enter_context(nc.sbuf_tensor("wbc", [128, TC], F32))
        wrow = mes.enter_context(nc.sbuf_tensor("wrow", [1, TC], BF16))

        n12 = 0
        n2 = 0
        nhe = 0
        nxa = 0
        for ei, e in enumerate(experts):
            if tick:
                tick(ei)
            if e >= 0:
                c.wait(nc.tensor, "moe_xa", nxa)
                c.wait(nc.sync, "g_trow")
                if e >= 1:
                    c.wait(nc.sync, "wbc_mm", e * 4)
                c.inc(nc.sync.dma_start(out=wrow[:, :],
                                        in_=wn_dram[e:e + 1, :]),
                      "wrow_ld", 16)
                c.wait(nc.tensor, "wrow_ld")
                for tchunk in range(4):
                    c.inc(nc.tensor.matmul(
                        P[4][:], c.ones[0:1, :],
                        wrow[0:1, tchunk * 512:(tchunk + 1) * 512],
                        start=True, stop=True), "wbc_mm", 1)
                    c.wait(nc.scalar, "wbc_mm")
                    c.inc(nc.scalar.activation(
                        wbc[:, tchunk * 512:(tchunk + 1) * 512], P[4][:],
                        AF.Copy), "wbc_ev", 1)
                    c.wait(nc.tensor, "wbc_ev")
            for th in range(2):
                t0 = th * 1024
                omm_prev = c.cnt.get("moe_omm", 0)
                for mg in range(NDT):
                    slot = (n12 // 2) % 2
                    if n12 >= 4:
                        c.wait(nc.gpsimd, "moe_hmm", (n12 // 2 - 1) * 64)
                    src1 = (Ws1 if e < 0 else W1[e])
                    srcg = (Wsg if e < 0 else Wg[e])
                    c.inc(nc.gpsimd.dma_start(
                        out=w1s[:, slot * 2048:(slot + 1) * 2048]
                        .rearrange("p (kt m) -> p kt m", m=128),
                        in_=src1[:, mg * 128:(mg + 1) * 128]
                        .rearrange("(kt p) m -> p kt m", p=128)),
                        "moe_w12", 16)
                    c.inc(nc.gpsimd.dma_start(
                        out=wgs[:, slot * 2048:(slot + 1) * 2048]
                        .rearrange("p (kt m) -> p kt m", m=128),
                        in_=srcg[:, mg * 128:(mg + 1) * 128]
                        .rearrange("(kt p) m -> p kt m", p=128)),
                        "moe_w12", 16)
                    n12 += 2
                    c.wait(nc.tensor, "moe_w12")
                    for tc2 in range(2):
                        tt0 = t0 + tc2 * 512
                        pa, pb = P[tc2 * 2], P[tc2 * 2 + 1]
                        if nhe >= 2:
                            c.wait(nc.tensor, "moe_hev", nhe - 1)
                        for kt in range(NDT):
                            c.inc(nc.tensor.matmul(
                                pa[:],
                                w1s[:, slot * 2048 + kt * 128:
                                    slot * 2048 + (kt + 1) * 128],
                                xn_sb[:, kt * TC + tt0: kt * TC + tt0 + 512],
                                start=(kt == 0), stop=(kt == NDT - 1)),
                                "moe_hmm", 1)
                            c.inc(nc.tensor.matmul(
                                pb[:],
                                wgs[:, slot * 2048 + kt * 128:
                                    slot * 2048 + (kt + 1) * 128],
                                xn_sb[:, kt * TC + tt0: kt * TC + tt0 + 512],
                                start=(kt == 0), stop=(kt == NDT - 1)),
                                "moe_hmm", 1)
                        c.wait(nc.scalar, "moe_hmm")
                        ssl = sil[:, (nhe % 2) * 512:(nhe % 2 + 1) * 512]
                        if nhe >= 2:
                            c.wait(nc.scalar, "moe_v", nhe - 1)
                        c.inc(nc.scalar.activation(ssl, pa[:], AF.Silu),
                              "moe_sc", 1)
                        c.wait(nc.vector, "moe_sc")
                        if nhe == 0 or (mg == 0 and tc2 == 0):
                            c.wait(nc.vector, "moe_omm", omm_prev)
                        hsl = h_sb[:, mg * 1024 + tc2 * 512:
                                   mg * 1024 + (tc2 + 1) * 512]
                        c.inc(nc.vector.tensor_tensor(hgc[:], ssl,
                                                      pb[:], OP.mult),
                              "moe_v", 1)
                        if e >= 0:
                            c.inc(nc.vector.tensor_tensor(
                                hsl, hgc[:], wbc[:, tt0:tt0 + 512],
                                OP.mult), "moe_hev", 1)
                        else:
                            c.inc(nc.vector.tensor_copy(hsl, hgc[:]),
                                  "moe_hev", 1)
                        nhe += 1
                c.wait(nc.tensor, "moe_hev")
                for dt in range(NDT):
                    slot2 = n2 % 2
                    if n2 >= 2:
                        c.wait(nc.gpsimd, "moe_omm", (n2 - 1) * 32)
                    src2 = (Ws2 if e < 0 else W2[e])
                    c.inc(nc.gpsimd.dma_start(
                        out=w2s[:, slot2 * 2048:(slot2 + 1) * 2048]
                        .rearrange("p (kt m) -> p kt m", m=128),
                        in_=src2[:, dt * 128:(dt + 1) * 128]
                        .rearrange("(kt p) m -> p kt m", p=128)),
                        "moe_w2", 16)
                    n2 += 1
                    c.wait(nc.tensor, "moe_w2")
                    for tc2 in range(2):
                        tt0 = t0 + tc2 * 512
                        pc = P[4 + nxa % 2]
                        if nxa >= 2:
                            c.wait(nc.tensor, "moe_xa", nxa - 1)
                        for kt in range(NDT):
                            c.inc(nc.tensor.matmul(
                                pc[:],
                                w2s[:, slot2 * 2048 + kt * 128:
                                    slot2 * 2048 + (kt + 1) * 128],
                                h_sb[:, kt * 1024 + tc2 * 512:
                                     kt * 1024 + (tc2 + 1) * 512],
                                start=(kt == 0), stop=(kt == NDT - 1)),
                                "moe_omm", 1)
                        c.wait(nc.vector, "moe_omm")
                        xsl = x_sb[:, dt * TC + tt0: dt * TC + tt0 + 512]
                        c.inc(nc.vector.tensor_tensor(xsl, xsl, pc[:],
                                                      OP.add), "moe_xa", 1)
                        nxa += 1
    c.barrier(("moe_xa", "moe_omm", "moe_hev"))


def _attn(c, kg, vg, q_dram, o_sb, mask_sb, bv_col,
          psS0, psS1, psO, psSum, kh_sb, vh_sb, q_sb, pt,
          sums_acc, rcp, tls, cc_sems):
    nc = c.nc
    b = c.snap("at_lq", "at_lkv", "at_smm", "at_ex", "at_pt", "at_sum",
               "at_bmm", "at_od")
    for nm, v in cc_sems:
        c.wait(nc.sync, nm, v)
    c.wait(nc.sync, "qkv_st")

    def load_q(h):
        slot = h % 2
        if h >= 2:
            c.wait(nc.sync, "at_od", b["at_od"] + (h - 1) * 4)
        c.inc(nc.sync.dma_start(
            out=q_sb[:, slot * TC:(slot + 1) * TC],
            in_=q_dram[h * 128:(h + 1) * 128, :]), "at_lq", 16)

    def load_kv(h):
        slot = h % 2
        khs = kh_sb[:, slot * 2 * TC:(slot + 1) * 2 * TC]
        vhs = vh_sb[:, slot * 2 * TC:(slot + 1) * 2 * TC]
        if h >= 2:
            c.wait(nc.sync, "at_od", b["at_od"] + (h - 1) * 4)
        c.inc(nc.sync.dma_start(
            out=khs[:, 0:TC], in_=kg[h * 128:(h + 1) * 128, :]), "at_lkv", 16)
        c.inc(nc.sync.dma_start(
            out=khs[:, TC:2 * TC],
            in_=kg[(NH + h) * 128:(NH + h + 1) * 128, :]), "at_lkv", 16)
        c.inc(nc.sync.dma_start(
            out=vhs[:, 0:TC].rearrange("p (s m) -> p s m", m=128),
            in_=vg[0:TC, h * 128:(h + 1) * 128]
            .rearrange("(s p) m -> p s m", p=128)), "at_lkv", 16)
        c.inc(nc.sync.dma_start(
            out=vhs[:, TC:2 * TC].rearrange("p (s m) -> p s m", m=128),
            in_=vg[TC:2 * TC, h * 128:(h + 1) * 128]
            .rearrange("(s p) m -> p s m", p=128)), "at_lkv", 16)

    load_q(0)
    load_kv(0)
    load_q(1)
    load_kv(1)
    nsl = 0
    for h in range(NH):
        if h + 2 < NH:
            load_q(h + 2)
            load_kv(h + 2)
        slot = h % 2
        qh = q_sb[:, slot * TC:(slot + 1) * TC]
        khs = kh_sb[:, slot * 2 * TC:(slot + 1) * 2 * TC]
        vhs = vh_sb[:, slot * 2 * TC:(slot + 1) * 2 * TC]
        c.wait(nc.tensor, "at_lq", b["at_lq"] + (h + 1) * 16)
        c.wait(nc.tensor, "at_lkv", b["at_lkv"] + (h + 1) * 64)
        for jc in range(4):
            S = 8 * jc + 8
            for s in range(S):
                ko = (s % 2) * TC + (s // 2) * 128
                ps = (psS0, psS1)[nsl % 2]
                if nsl >= 2:
                    c.wait(nc.tensor, "at_ex", b["at_ex"] + nsl - 1)
                c.inc(nc.tensor.matmul(
                    ps[:], khs[:, ko:ko + 128],
                    qh[:, jc * 512:(jc + 1) * 512],
                    start=True, stop=True), "at_smm", 1)
                ptile = pt[:, (nsl % 3) * 512:(nsl % 3 + 1) * 512]
                c.wait(nc.scalar, "at_smm")
                if nsl >= 3:
                    c.wait(nc.scalar, "at_pt", b["at_pt"] + nsl - 2)
                    c.wait(nc.scalar, "at_sum", b["at_sum"] + nsl - 2)
                c.inc(nc.scalar.activation(
                    ptile, ps[:], AF.Exp, scale=ISC,
                    bias=c.mneg10[:, 0:1]), "at_ex", 1)
                if s >= S - 8:
                    d = s - (S - 8)
                    c.wait(nc.vector, "at_ex")
                    c.inc(nc.vector.tensor_tensor(
                        ptile, ptile, mask_sb[:, d * 512:(d + 1) * 512],
                        OP.mult), "at_msk", 1)
                    gate = ("at_msk", c.cnt["at_msk"])
                else:
                    c.wait(nc.vector, "at_ex")
                    gate = ("at_ex", c.cnt["at_ex"])
                if s == 0:
                    c.wait(nc.vector, "at_bmm")
                    c.inc(nc.vector.tensor_copy(sums_acc[:], ptile),
                          "at_sum", 1)
                else:
                    c.inc(nc.vector.tensor_tensor(
                        sums_acc[:], sums_acc[:], ptile, OP.add), "at_sum", 1)
                c.wait(nc.tensor, gate[0], gate[1])
                c.inc(nc.tensor.matmul(
                    psO[:], vhs[:, ko:ko + 128], ptile,
                    start=(s == 0), stop=(s == S - 1)), "at_pt", 1)
                nsl += 1
            c.wait(nc.tensor, "at_ln")
            c.wait(nc.tensor, "at_sum")
            c.inc(nc.tensor.matmul(psSum[:], c.ones32f[:], sums_acc[:],
                                   start=True, stop=True), "at_bmm", 1)
            c.wait(nc.scalar, "at_bmm")
            c.wait(nc.scalar, "at_od")
            c.inc(nc.scalar.activation(tls[:], psSum[:], AF.Ln), "at_ln", 1)
            c.wait(nc.scalar, "at_ln")
            c.inc(nc.scalar.activation(rcp[:], tls[:], AF.Exp, scale=-1.0),
                  "at_rc", 1)
            c.wait(nc.vector, "at_rc")
            c.wait(nc.vector, "at_pt")
            osl = o_sb[:, h * TC + jc * 512: h * TC + (jc + 1) * 512]
            c.inc(nc.vector.tensor_tensor(tls[:], psO[:], rcp[:], OP.mult),
                  "at_on", 1)
            if bv_col is not None:
                c.inc(nc.vector.tensor_scalar(
                    osl, tls[:], bv_col[:, h:h + 1], None, OP.add),
                    "at_od", 1)
            else:
                c.inc(nc.vector.tensor_copy(osl, tls[:]), "at_od", 1)
            c.wait(nc.tensor, "at_od")
    c.barrier(("at_od", "at_sum", "at_msk"))


def _outproj(c, woT, o_sb, x_sb, P, bo_col):
    nc = c.nc
    b = c.snap("op_w", "op_mm", "op_xa")
    with contextlib.ExitStack() as wes:
        wst = wes.enter_context(nc.sbuf_tensor(_un("wst2"), [128, 2 * 16 * 128],
                                               BF16))
        for DT in range(NDT):
            slot = DT % 2
            wsl = wst[:, slot * 2048:(slot + 1) * 2048]
            if DT >= 2:
                c.wait(nc.gpsimd, "op_mm", b["op_mm"] + (DT - 1) * 64)
            c.inc(nc.gpsimd.dma_start(
                out=wsl.rearrange("p (kt m) -> p kt m", m=128),
                in_=woT[:, DT * 128:(DT + 1) * 128]
                .rearrange("(kt p) m -> p kt m", p=128)), "op_w", 16)
            c.wait(nc.tensor, "op_w")
            for tch in range(4):
                pp = P[4 + tch % 2]
                it = DT * 4 + tch
                if it >= 2:
                    c.wait(nc.tensor, "op_xa", b["op_xa"] + it - 1)
                for kt in range(NDT):
                    c.inc(nc.tensor.matmul(
                        pp[:], wsl[:, kt * 128:(kt + 1) * 128],
                        o_sb[:, kt * TC + tch * 512:
                             kt * TC + tch * 512 + 512],
                        start=(kt == 0), stop=(kt == NDT - 1)), "op_mm", 1)
                c.wait(nc.vector, "op_mm")
                xsl = x_sb[:, DT * TC + tch * 512: DT * TC + tch * 512 + 512]
                if bo_col is not None:
                    c.inc(nc.vector.scalar_tensor_tensor(
                        xsl, pp[:], bo_col[:, DT:DT + 1], xsl, OP.add,
                        OP.add), "op_xa", 1)
                else:
                    c.inc(nc.vector.tensor_tensor(xsl, xsl, pp[:],
                                                  OP.add), "op_xa", 1)
    c.barrier(("op_xa",))


def _l1ffn(c, wguT, wdnT, h_dram, xn_es, xn_sb, x_sb, P, eps6_ap):
    nc = c.nc
    with contextlib.ExitStack() as ges:
        wga = ges.enter_context(nc.sbuf_tensor("wga", [128, 2 * 16 * 128], BF16))
        wgb = ges.enter_context(nc.sbuf_tensor("wgb", [128, 2 * 16 * 128], BF16))
        ssl2 = ges.enter_context(nc.sbuf_tensor("ssl2", [128, 2 * 512], F32))
        hst = ges.enter_context(nc.sbuf_tensor("hst", [128, 3 * 512], BF16))
        c.wait(nc.sync, "cc_gu", 1)
        c.wait(nc.sync, "cc_k1", 1)
        c.wait(nc.sync, "cc_v1", 1)
        nh = 0
        for j in range(44):
            slot = j % 2
            wg_ = wga[:, slot * 2048:(slot + 1) * 2048]
            wu_ = wgb[:, slot * 2048:(slot + 1) * 2048]
            if j >= 2:
                c.wait(nc.sync, "gu_mm", (j - 1) * 128)
            c.inc(nc.sync.dma_start(
                out=wg_.rearrange("p (kt m) -> p kt m", m=128),
                in_=wguT[:, j * 128:(j + 1) * 128]
                .rearrange("(kt p) m -> p kt m", p=128)), "gu_w", 16)
            c.inc(nc.sync.dma_start(
                out=wu_.rearrange("p (kt m) -> p kt m", m=128),
                in_=wguT[:, (44 + j) * 128:(45 + j) * 128]
                .rearrange("(kt p) m -> p kt m", p=128)), "gu_w", 16)
            c.wait(nc.tensor, "gu_w")
            for tch in range(4):
                pa, pb = P[(tch % 2) * 2], P[(tch % 2) * 2 + 1]
                if nh >= 2:
                    c.wait(nc.tensor, "gu_hv", nh - 1)
                for kt in range(NDT):
                    c.inc(nc.tensor.matmul(
                        pa[:], wg_[:, kt * 128:(kt + 1) * 128],
                        xn_sb[:, kt * TC + tch * 512:
                              kt * TC + tch * 512 + 512],
                        start=(kt == 0), stop=(kt == NDT - 1)), "gu_mm", 1)
                    c.inc(nc.tensor.matmul(
                        pb[:], wu_[:, kt * 128:(kt + 1) * 128],
                        xn_sb[:, kt * TC + tch * 512:
                              kt * TC + tch * 512 + 512],
                        start=(kt == 0), stop=(kt == NDT - 1)), "gu_mm", 1)
                c.wait(nc.scalar, "gu_mm")
                sslc = ssl2[:, (nh % 2) * 512:(nh % 2 + 1) * 512]
                if nh >= 2:
                    c.wait(nc.scalar, "gu_hv", nh - 1)
                c.inc(nc.scalar.activation(sslc, pa[:], AF.Silu), "gu_sc", 1)
                c.wait(nc.vector, "gu_sc")
                hsl = hst[:, (nh % 3) * 512:(nh % 3 + 1) * 512]
                if nh >= 3:
                    c.wait(nc.vector, "gu_st", (nh - 2) * 16)
                c.inc(nc.vector.tensor_tensor(hsl, sslc, pb[:],
                                              OP.mult), "gu_hv", 1)
                c.wait(nc.sync, "gu_hv")
                c.inc(nc.sync.dma_start(
                    out=h_dram[j * 128:(j + 1) * 128,
                               tch * 512:(tch + 1) * 512],
                    in_=hsl), "gu_st", 16)
                nh += 1
    c.barrier(("gu_st", "gu_hv"))
    xn_es.close()

    with contextlib.ExitStack() as des:
        h_sb = des.enter_context(nc.sbuf_tensor("h44", [128, 44 * 512], BF16))
        wdns = des.enter_context(nc.sbuf_tensor("wdns", [128, 2 * 44 * 128],
                                                BF16))
        y_sb = des.enter_context(nc.sbuf_tensor("y_sb", [128, 16 * 512], BF16))
        sqst = des.enter_context(nc.sbuf_tensor("sqst", [128, 2 * 512], BF16))
        rstd = des.enter_context(nc.sbuf_tensor("rstd2", [128, 512], F32))
        tmp2 = des.enter_context(nc.sbuf_tensor("tmp2", [128, 512], F32))
        c.wait(nc.sync, "cc_dn", 1)
        ndq = 0
        for tau in range(4):
            t0 = tau * 512
            c.wait(nc.scalar, "rm_xa")
            c.wait(nc.sync, "dn_ev", c.cnt.get("dn_ev", 0))
            c.inc(nc.sync.dma_start(
                out=h_sb[:].rearrange("p (kt m) -> p kt m", m=512),
                in_=h_dram[:, t0:t0 + 512]
                .rearrange("(kt p) m -> p kt m", p=128)), "dn_h", 16)
            for dt in range(NDT):
                slot = ndq % 2
                wd_ = wdns[:, slot * 44 * 128:(slot + 1) * 44 * 128]
                if ndq >= 2:
                    c.wait(nc.sync, "dn_mm", (ndq - 1) * 44)
                c.inc(nc.sync.dma_start(
                    out=wd_.rearrange("p (kt m) -> p kt m", m=128),
                    in_=wdnT[:, dt * 128:(dt + 1) * 128]
                    .rearrange("(kt p) m -> p kt m", p=128)), "dn_w", 16)
                c.wait(nc.tensor, "dn_w")
                c.wait(nc.tensor, "dn_h")
                pp = P[4 + ndq % 2]
                if ndq >= 2:
                    c.wait(nc.tensor, "dn_ev", ndq - 1)
                for kt in range(44):
                    c.inc(nc.tensor.matmul(
                        pp[:], wd_[:, kt * 128:(kt + 1) * 128],
                        h_sb[:, kt * 512:(kt + 1) * 512],
                        start=(kt == 0), stop=(kt == 43)), "dn_mm", 1)
                c.wait(nc.scalar, "dn_mm")
                c.inc(nc.scalar.activation(
                    y_sb[:, dt * 512:(dt + 1) * 512], pp[:], AF.Copy),
                    "dn_ev", 1)
                ndq += 1
            for dt in range(NDT):
                sq = sqst[:, (dt % 2) * 512:(dt % 2 + 1) * 512]
                if dt >= 2:
                    c.wait(nc.scalar, "rm_mm", c.cnt["rm_mm"] - 1)
                c.inc(nc.scalar.activation(
                    sq, y_sb[:, dt * 512:(dt + 1) * 512], AF.Square),
                    "rm_sq", 1)
                c.wait(nc.tensor, "rm_sq")
                c.inc(nc.tensor.matmul(P[0][:], c.ones[:], sq,
                                       start=(dt == 0),
                                       stop=(dt == NDT - 1)), "rm_mm", 1)
            c.wait(nc.scalar, "rm_mm")
            c.inc(nc.scalar.activation(tmp2[:], P[0][:], AF.Ln,
                                       scale=1.0 / D, bias=eps6_ap),
                  "rm_ln", 1)
            c.wait(nc.scalar, "rm_ln")
            c.inc(nc.scalar.activation(rstd[:], tmp2[:], AF.Exp,
                                       scale=-0.5), "rm_rs", 1)
            c.wait(nc.vector, "rm_rs")
            for dt in range(NDT):
                xsl = x_sb[:, dt * TC + t0: dt * TC + t0 + 512]
                c.inc(nc.vector.tensor_tensor(
                    tmp2[:], y_sb[:, dt * 512:(dt + 1) * 512], rstd[:],
                    OP.mult), "rm_v", 1)
                c.wait(nc.vector, "rm_v")
                c.inc(nc.vector.tensor_tensor(xsl, xsl, tmp2[:], OP.add),
                      "rm_xa", 1)
            c.wait(nc.tensor, "rm_xa")
    c.barrier(("rm_xa", "dn_ev"))



def build_fused(debug=False, use_cc=True, cc_early=True,
                cc_serial=True):
    nc = bass.Bass("TRN2", target_bir_lowering=False)
    c = Ctx(nc)

    xT = nc.dram_tensor("xT", [D, TC], F32, kind="ExternalInput").ap()
    wqkvT = nc.dram_tensor("wqkvT", [D, 3 * D], F32, kind="ExternalInput").ap()
    bqkv = nc.dram_tensor("bqkv", [128, 48], F32, kind="ExternalInput").ap()
    bqkvS = nc.dram_tensor("bqkvS", [128, 48], F32, kind="ExternalInput").ap()
    ropeC = nc.dram_tensor("ropeC", [128, TC], F32, kind="ExternalInput").ap()
    ropeS = nc.dram_tensor("ropeS", [128, TC], F32, kind="ExternalInput").ap()
    W1 = nc.dram_tensor("W1", [7, D, D], F32, kind="ExternalInput").ap()
    Wg = nc.dram_tensor("Wg", [7, D, D], F32, kind="ExternalInput").ap()
    W2 = nc.dram_tensor("W2", [7, D, D], F32, kind="ExternalInput").ap()
    Ws1 = nc.dram_tensor("Ws1", [D, D], F32, kind="ExternalInput").ap()
    Wsg = nc.dram_tensor("Wsg", [D, D], F32, kind="ExternalInput").ap()
    Ws2 = nc.dram_tensor("Ws2", [D, D], F32, kind="ExternalInput").ap()
    gate_w = nc.dram_tensor("gate_w", [D, 7], F32, kind="ExternalInput").ap()
    gate_b = nc.dram_tensor("gate_b", [1, 7], F32, kind="ExternalInput").ap()
    gbias = nc.dram_tensor("gbias", [1, 7], F32, kind="ExternalInput").ap()
    woT0 = nc.dram_tensor("woT0", [D, D], F32, kind="ExternalInput").ap()
    bo_d = nc.dram_tensor("bo_d", [128, 16], F32, kind="ExternalInput").ap()
    bv_d = nc.dram_tensor("bv_d", [128, 16], F32, kind="ExternalInput").ap()
    wprojT = nc.dram_tensor("wprojT", [D, 3 * D], F32,
                            kind="ExternalInput").ap()
    woT1 = nc.dram_tensor("woT1", [D, D], F32, kind="ExternalInput").ap()
    maskD = nc.dram_tensor("maskD", [128, 8 * 512], BF16,
                           kind="ExternalInput").ap()
    identD = nc.dram_tensor("identD", [128, 128], BF16,
                            kind="ExternalInput").ap()
    swapD = nc.dram_tensor("swapD", [128, 128], BF16,
                           kind="ExternalInput").ap()
    gup_pk = nc.dram_tensor("gup_pk", [128, 2816], I32,
                            kind="ExternalInput").ap()
    gup_sc = nc.dram_tensor("gup_sc", [8, 2816], F16,
                            kind="ExternalInput").ap()
    selg = nc.dram_tensor("selg", [8, 128], F32, kind="ExternalInput").ap()
    dn_pk = nc.dram_tensor("dn_pk", [88, 2048], I32,
                           kind="ExternalInput").ap()
    dn_sc = nc.dram_tensor("dn_sc", [8, 2048], F16, kind="ExternalInput").ap()
    seld = nc.dram_tensor("seld", [8, 88], F32, kind="ExternalInput").ap()

    q_dram = nc.dram_tensor("q_dram", [D, TC], BF16, kind="Internal").ap()
    k_src = [nc.dram_tensor(f"k_src{l}", [D, TC], BF16, kind="Internal").ap()
             for l in range(2)]
    v_src = [nc.dram_tensor(f"v_src{l}", [TC, D], BF16, kind="Internal").ap()
             for l in range(2)]
    kg = [nc.dram_tensor(f"kg{l}", [2 * D, TC], BF16, kind="Internal").ap()
          for l in range(2)]
    vg = [nc.dram_tensor(f"vg{l}", [2 * TC, D], BF16, kind="Internal").ap()
          for l in range(2)]
    wn_dram = nc.dram_tensor("wn_dram", [7, TC], BF16, kind="Internal").ap()
    h_dram = nc.dram_tensor("h_dram", [5632, TC], BF16, kind="Internal").ap()
    gu_scr = nc.dram_tensor("gu_scr", [128, 22528], BF16, kind="Internal").ap()
    gu_src = nc.dram_tensor("gu_src", [256, 11264], BF16, kind="Internal").ap()
    dn_src = nc.dram_tensor("dn_src", [704, 2048], BF16, kind="Internal").ap()
    wguT = nc.dram_tensor("wguT", [2048, 11264], BF16,
                          kind="Internal").ap()
    wdnT = nc.dram_tensor("wdnT", [5632, 2048], BF16,
                          kind="Internal").ap()

    out_xT = nc.dram_tensor("out_xT", [D, TC], F32, kind="ExternalOutput").ap()
    if debug:
        d_w1c = nc.dram_tensor("d_w1c", [D, D], F32, kind="ExternalOutput").ap()
        d_wgc = nc.dram_tensor("d_wgc", [D, D], F32, kind="ExternalOutput").ap()
        d_w2c = nc.dram_tensor("d_w2c", [D, D], F32, kind="ExternalOutput").ap()
        dbg = {nm: nc.dram_tensor(nm, shp, BF16, kind="ExternalOutput").ap()
               for nm, shp in [
                   ("d_xn0", [D, TC]), ("d_q0", [D, TC]), ("d_k0", [D, TC]),
                   ("d_v0", [TC, D]), ("d_xm0", [D, TC]), ("d_o0", [D, TC]),
                   ("d_x1", [D, TC]), ("d_xf1", [D, TC]), ("d_o1", [D, TC]),
                   ("d_wn", [7, TC]),
               ]}

        def dump_sb(nm, sb):
            c.inc(nc.sync.dma_start(
                out=dbg[nm].rearrange("(dt p) t -> p dt t", p=128),
                in_=sb[:].rearrange("p (dt t) -> p dt t", t=TC)), "dbg", 16)
            c.wait(nc.sync, "dbg")

        def dump_dram(nm, src):
            c.inc(nc.sync.dma_start(out=dbg[nm][:, :], in_=src[:, :]),
                  "dbg", 16)
            c.wait(nc.sync, "dbg")

    PAIRS = [[0, 1], [2, 3], [4, 5], [6, 7]]

    with c.es:
        x_sb = c.sb("x_sb", [128, NDT * TC], BF16)
        c.ones = c.sb("ones", [128, 128], BF16)
        c.ident = c.sb("ident", [128, 128], BF16)
        c.swap_sb = c.sb("swap_sb", [128, 128], BF16)
        c.ones32 = c.sb("ones32", [1, 128], F32)
        c.ones32f = c.sb("ones32f", [128, 128], F32)
        c.mneg10 = c.sb("mneg10", [128, 1], F32)
        eps_sb = c.sb("eps_sb", [128, 1], F32)
        eps6_sb = c.sb("eps6_sb", [128, 1], F32)
        c.bq_sb = c.sb("bq_sb", [128, 48], F32)
        c.bqs_sb = c.sb("bqs_sb", [128, 48], F32)
        c.gb_bc = c.sb("gb_bc", [128, 7], F32)
        c.bb_bc = c.sb("bb_bc", [128, 7], F32)
        bo_sb = c.sb("bo_sb", [128, 16], F32)
        bv_sb = c.sb("bv_sb", [128, 16], F32)

        P = [c.ps(f"P{i}") for i in range(6)]
        psS = c.ps("psS", (128, 128))

        blk = c.es.enter_context(nc.Block())

        c.inc(nc.gpsimd.dma_start(
            out=x_sb[:].rearrange("p (dt t) -> p dt t", t=TC),
            in_=xT.rearrange("(dt p) t -> p dt t", p=128)), "s_x", 16)
        c.inc(nc.gpsimd.memset(c.ones[:], 1.0), "s_x", 1)
        c.inc(nc.gpsimd.memset(c.ones32[:], 1.0), "s_x", 1)
        c.inc(nc.gpsimd.memset(c.ones32f[:], 1.0), "s_x", 1)
        c.inc(nc.gpsimd.memset(c.mneg10[:], EXPB), "s_x", 1)
        c.inc(nc.gpsimd.memset(eps_sb[:], 1e-5), "s_x", 1)
        c.inc(nc.gpsimd.memset(eps6_sb[:], 1e-6), "s_x", 1)
        c.inc(nc.sync.dma_start(out=c.ident[:], in_=identD[:, :]), "s_x", 16)
        c.inc(nc.sync.dma_start(out=c.swap_sb[:], in_=swapD[:, :]), "s_x", 16)
        c.inc(nc.sync.dma_start(out=c.bq_sb[:], in_=bqkv[:, :]), "s_x", 16)
        c.inc(nc.sync.dma_start(out=c.bqs_sb[:], in_=bqkvS[:, :]), "s_x", 16)
        c.inc(nc.sync.dma_start(out=bo_sb[:], in_=bo_d[:, :]), "s_x", 16)
        c.inc(nc.sync.dma_start(out=bv_sb[:], in_=bv_d[:, :]), "s_x", 16)

        _dequant(c, gup_pk, gup_sc, selg, dn_pk, dn_sc, seld,
                 gu_scr, gu_src, dn_src, P[0])
        c.dq_es.close()

        xn_es = contextlib.ExitStack()
        xn_sb = xn_es.enter_context(
            nc.sbuf_tensor("xn_sb", [128, NDT * TC], BF16))
        c.barrier(("s_x", "dq_st", "dq_v"))
        _ln(c, x_sb, xn_sb, P[0], P[1], eps_sb[:, 0:1])
        c.ln_es.close()
        c.barrier(("lnv",))
        if debug:
            dump_sb("d_xn0", xn_sb)

        _qkv(c, wqkvT, q_dram, k_src[0], v_src[0], xn_sb, P, rope=True,
             ropeC=ropeC, ropeS=ropeS)
        if debug:
            dump_dram("d_q0", q_dram)
            dump_dram("d_k0", k_src[0])
            dump_dram("d_v0", v_src[0])

        def cc_step(i):
            if not use_cc:
                return
            if i == 0:
                for nm in ("s_x", "dq_ld", "dq_st", "s_rope", "qkv_w",
                           "qkv_st", "v_w", "g_w", "g_trow", "wrow_ld",
                           "moe_w12", "moe_w2"):
                    c.wait(nc.gpsimd, nm)
                c.inc(nc.gpsimd.collective_compute(
                    "AllGather", mybir.AluOpType.bypass,
                    replica_groups=PAIRS,
                    ins=[k_src[0][:, :].opt()], outs=[kg[0][:, :].opt()]),
                    "cc_k0", 1)
            elif i == 1:
                c.wait(nc.gpsimd, "cc_k0", 1)
                c.inc(nc.gpsimd.collective_compute(
                    "AllGather", mybir.AluOpType.bypass,
                    replica_groups=PAIRS,
                    ins=[v_src[0][:, :].opt()], outs=[vg[0][:, :].opt()]),
                    "cc_v0", 1)
            elif i == 2:
                c.wait(nc.gpsimd, "cc_v0", 1)
                c.inc(nc.gpsimd.collective_compute(
                    "AllGather", mybir.AluOpType.bypass,
                    replica_groups=[list(range(NC))],
                    ins=[gu_src[:, :].opt()], outs=[wguT[:, :].opt()]),
                    "cc_gu", 1)
            elif i == 3:
                c.wait(nc.gpsimd, "cc_gu", 1)
                c.inc(nc.gpsimd.collective_compute(
                    "AllGather", mybir.AluOpType.bypass,
                    replica_groups=[list(range(NC))],
                    ins=[dn_src[:, :].opt()], outs=[wdnT[:, :].opt()]),
                    "cc_dn", 1)

        def emit_cc_l0():
            c.wait(nc.gpsimd, "qkv_st")
            c.wait(nc.gpsimd, "dq_st")
            if use_cc:
                for i in range(4):
                    cc_step(i)
            else:
                c.inc(nc.gpsimd.dma_start(out=kg[0][0:D, :],
                                          in_=k_src[0][:, :]), "cc_k0", 16)
                c.inc(nc.gpsimd.dma_start(out=kg[0][D:2 * D, :],
                                          in_=k_src[0][:, :]), "cc_k0", 16)
                c.inc(nc.gpsimd.dma_start(out=vg[0][0:TC, :],
                                          in_=v_src[0][:, :]), "cc_v0", 16)
                c.inc(nc.gpsimd.dma_start(out=vg[0][TC:2 * TC, :],
                                          in_=v_src[0][:, :]), "cc_v0", 16)
                for r in range(NC):
                    c.inc(nc.gpsimd.dma_start(
                        out=wguT[r * 256:(r + 1) * 256, :],
                        in_=gu_src[:, :]), "cc_gu", 16)
                    c.inc(nc.gpsimd.dma_start(
                        out=wdnT[r * 704:(r + 1) * 704, :],
                        in_=dn_src[:, :]), "cc_dn", 16)

        if cc_early and not use_cc:
            emit_cc_l0()
        _gating(c, gate_w, gate_b, gbias, xn_sb, wn_dram, psS, P)
        if debug:
            dump_dram("d_wn", wn_dram)
        _moe(c, W1, Wg, W2, Ws1, Wsg, Ws2, wn_dram, xn_sb, x_sb, P)
        xn_es.close()
        if use_cc:
            for i in range(4):
                cc_step(i)
            c.wait(nc.gpsimd, "cc_dn", 1)

        if not cc_early:
            emit_cc_l0()
        if debug:
            dump_sb("d_xm0", x_sb)

        with contextlib.ExitStack() as aes:
            o_sb = aes.enter_context(
                nc.sbuf_tensor("o_sb", [128, NDT * TC], BF16))
            q_sb = aes.enter_context(nc.sbuf_tensor("q_sb", [128, 2 * TC],
                                                    BF16))
            kh_sb = aes.enter_context(
                nc.sbuf_tensor("kh_sb", [128, 2 * 2 * TC], BF16))
            vh_sb = aes.enter_context(
                nc.sbuf_tensor("vh_sb", [128, 2 * 2 * TC], BF16))
            pt = aes.enter_context(nc.sbuf_tensor("pt", [128, 3 * 512], BF16))
            sums_acc = aes.enter_context(
                nc.sbuf_tensor("sums_acc", [128, 512], F32))
            rcp = aes.enter_context(nc.sbuf_tensor("rcp", [128, 512], F32))
            tls = aes.enter_context(nc.sbuf_tensor("tls", [128, 512], F32))
            mask_sb = aes.enter_context(
                nc.sbuf_tensor("mask_sb", [128, 8 * 512], BF16))
            c.inc(nc.sync.dma_start(out=mask_sb[:], in_=maskD[:, :]),
                  "at_mld", 16)
            c.wait(nc.vector, "at_mld")
            _attn(c, kg[0], vg[0], q_dram, o_sb, mask_sb, bv_sb,
                  P[0], P[1], P[2], P[3], kh_sb, vh_sb, q_sb, pt,
                  sums_acc, rcp, tls,
                  [("cc_k0", 1), ("cc_v0", 1), ("cc_gu", 1), ("cc_dn", 1)])
            if debug:
                dump_sb("d_o0", o_sb)
            c.wait(nc.gpsimd, "cc_gu", 1)
            c.wait(nc.gpsimd, "cc_dn", 1)
            _outproj(c, woT0, o_sb, x_sb, P, bo_sb)
        if debug:
            dump_sb("d_x1", x_sb)

        xn_es = contextlib.ExitStack()
        xn_sb = xn_es.enter_context(
            nc.sbuf_tensor("xn_sb1", [128, NDT * TC], BF16))
        _ln(c, x_sb, xn_sb, P[0], P[1], eps_sb[:, 0:1])
        c.ln_es.close()
        c.barrier(("lnv",))

        _qkv(c, wprojT, q_dram, k_src[1], v_src[1], xn_sb, P, rope=False)

        if use_cc:
            for nm in ("qkv_w", "v_w", "qkv_st"):
                c.wait(nc.gpsimd, nm)
            c.inc(nc.gpsimd.collective_compute(
                "AllGather", mybir.AluOpType.bypass, replica_groups=PAIRS,
                ins=[k_src[1][:, :].opt()], outs=[kg[1][:, :].opt()]),
                "cc_k1", 1)
            if cc_serial:
                c.wait(nc.gpsimd, "cc_k1", 1)
            c.inc(nc.gpsimd.collective_compute(
                "AllGather", mybir.AluOpType.bypass, replica_groups=PAIRS,
                ins=[v_src[1][:, :].opt()], outs=[vg[1][:, :].opt()]),
                "cc_v1", 1)
            c.wait(nc.gpsimd, "cc_v1", 1)
        else:
            c.inc(nc.gpsimd.dma_start(out=kg[1][0:D, :],
                                      in_=k_src[1][:, :]), "cc_k1", 16)
            c.inc(nc.gpsimd.dma_start(out=kg[1][D:2 * D, :],
                                      in_=k_src[1][:, :]), "cc_k1", 16)
            c.inc(nc.gpsimd.dma_start(out=vg[1][0:TC, :],
                                      in_=v_src[1][:, :]), "cc_v1", 16)
            c.inc(nc.gpsimd.dma_start(out=vg[1][TC:2 * TC, :],
                                      in_=v_src[1][:, :]), "cc_v1", 16)

        _l1ffn(c, wguT, wdnT, h_dram, xn_es, xn_sb, x_sb, P,
               eps6_sb[:, 0:1])
        if debug:
            dump_sb("d_xf1", x_sb)

        with contextlib.ExitStack() as aes:
            o_sb = aes.enter_context(
                nc.sbuf_tensor("o_sb1", [128, NDT * TC], BF16))
            q_sb = aes.enter_context(nc.sbuf_tensor("q_sb1", [128, 2 * TC],
                                                    BF16))
            kh_sb = aes.enter_context(
                nc.sbuf_tensor("kh_sb1", [128, 2 * 2 * TC], BF16))
            vh_sb = aes.enter_context(
                nc.sbuf_tensor("vh_sb1", [128, 2 * 2 * TC], BF16))
            pt = aes.enter_context(nc.sbuf_tensor("pt1", [128, 3 * 512], BF16))
            sums_acc = aes.enter_context(
                nc.sbuf_tensor("sums_acc1", [128, 512], F32))
            rcp = aes.enter_context(nc.sbuf_tensor("rcp1", [128, 512], F32))
            tls = aes.enter_context(nc.sbuf_tensor("tls1", [128, 512], F32))
            mask_sb = aes.enter_context(
                nc.sbuf_tensor("mask_sb1", [128, 8 * 512], BF16))
            c.inc(nc.sync.dma_start(out=mask_sb[:], in_=maskD[:, :]),
                  "at_mld", 16)
            c.wait(nc.vector, "at_mld")
            _attn(c, kg[1], vg[1], q_dram, o_sb, mask_sb, None,
                  P[0], P[1], P[2], P[3], kh_sb, vh_sb, q_sb, pt,
                  sums_acc, rcp, tls, [("cc_k1", 1), ("cc_v1", 1)])
            if debug:
                dump_sb("d_o1", o_sb)
            _outproj(c, woT1, o_sb, x_sb, P, None)

        c.wait(nc.gpsimd, "op_xa")
        c.inc(nc.gpsimd.dma_start(
            out=out_xT.rearrange("(dt p) t -> p dt t", p=128),
            in_=x_sb[:].rearrange("p (dt t) -> p dt t", t=TC)), "st_x", 16)
        c.wait(nc.gpsimd, "st_x")
        if debug:
            c.inc(nc.sync.dma_start(out=d_w1c[:, :], in_=W1[3]), "dbg", 16)
            c.inc(nc.sync.dma_start(out=d_wgc[:, :], in_=Wg[3]), "dbg", 16)
            c.inc(nc.sync.dma_start(out=d_w2c[:, :], in_=W2[3]), "dbg", 16)
            c.wait(nc.sync, "dbg")

    return nc



def _prep(inputs):
    cos, sin = _rope_tables()
    sinS = sin.copy()
    sinS[:, :64] *= -1.0
    wqkvT = np.ascontiguousarray(inputs["L0_attn_Wqkv"].T)
    bq = np.ascontiguousarray(inputs["L0_attn_bqkv"].reshape(48, 128).T)
    swp = np.concatenate([np.arange(64, 128), np.arange(64)])
    bqs = np.ascontiguousarray(bq[swp])
    swapM = np.zeros((128, 128), dtype=BF)
    swapM[np.arange(128), swp] = 1.0
    woT0 = np.ascontiguousarray(inputs["L0_attn_Wo"].T)
    bo = np.ascontiguousarray(inputs["L0_attn_bo"].reshape(16, 128).T)
    bv = np.ascontiguousarray(
        inputs["L0_attn_bqkv"][2 * D:].reshape(16, 128).T)
    wprojT = np.ascontiguousarray(inputs["L1_attn_w_project"].T)
    woT1 = np.ascontiguousarray(inputs["L1_attn_w_out"].T)
    masks = [_mask_table(0), _mask_table(1)]

    gupT = np.ascontiguousarray(inputs["L1_ffn_gate_up_packed"].T)
    gus = inputs["L1_ffn_gate_up_scales"]
    dnpT = np.ascontiguousarray(inputs["L1_ffn_down_packed"].T)
    dns = inputs["L1_ffn_down_scales"]
    selg = np.zeros((8, 128), np.float32)
    for p in range(128):
        selg[(p // 64) * 4 + (p % 4), p] = 1.0

    x = inputs["L0_attn_x"]
    maps = []
    for cid in range(NC):
        m, cls = cid // 2, cid % 2
        pos = _positions(cls)
        r = cid
        gpk = np.ascontiguousarray(
            gupT[32 * r:32 * r + 32].reshape(128, 2816))
        gsc = np.ascontiguousarray(gus[2 * r:2 * r + 2].reshape(8, 2816))
        dpk = np.ascontiguousarray(dnpT[88 * r:88 * r + 88])
        g0 = (704 * r) // 128
        dsc = np.zeros((8, 2048), np.float16)
        nsr = min(6, 44 - g0)
        dsc[:nsr] = dns[g0:g0 + nsr]
        sd = np.zeros((8, 88), np.float32)
        for cc in range(88):
            sd[(704 * r + 8 * cc) // 128 - g0, cc] = 1.0
        maps.append({
            "xT": np.ascontiguousarray(x[m][pos].T),
            "wqkvT": wqkvT, "bqkv": bq, "bqkvS": bqs,
            "ropeC": np.ascontiguousarray(cos[pos].T),
            "ropeS": np.ascontiguousarray(sinS[pos].T),
            "W1": inputs["L0_ffn_W1"], "Wg": inputs["L0_ffn_Wg"],
            "W2": inputs["L0_ffn_W2"], "Ws1": inputs["L0_ffn_Ws1"],
            "Wsg": inputs["L0_ffn_Wsg"], "Ws2": inputs["L0_ffn_Ws2"],
            "gate_w": inputs["L0_ffn_gate_w"],
            "gate_b": inputs["L0_ffn_gate_b"].reshape(1, 7),
            "gbias": inputs["L0_ffn_biases"].reshape(1, 7),
            "woT0": woT0, "bo_d": bo, "bv_d": bv,
            "wprojT": wprojT, "woT1": woT1,
            "maskD": masks[cls],
            "identD": np.eye(128, dtype=BF), "swapD": swapM,
            "gup_pk": gpk, "gup_sc": gsc, "selg": selg,
            "dn_pk": dpk, "dn_sc": dsc, "seld": sd,
        })
    return maps


def kernel(**inputs):
    inputs = {k: np.asarray(v) for k, v in inputs.items()}
    nc = build_fused()
    res = run_bass_kernel_spmd(nc, _prep(inputs), core_ids=list(range(NC)))
    out = np.empty((B, T, D), dtype=np.float32)
    for cid in range(NC):
        m, cls = cid // 2, cid % 2
        pos = _positions(cls)
        out[m][pos] = np.asarray(res.results[cid]["out_xT"]).T
    return out
